# version 11
# baseline (speedup 1.0000x reference)
import math
from contextlib import ExitStack

import numpy as np

import concourse.bass as bass
import concourse.mybir as mybir
from concourse.bass_utils import run_bass_kernel_spmd

F32 = mybir.dt.float32
BF16 = mybir.dt.bfloat16
AF = mybir.ActivationFunctionType
ALU = mybir.AluOpType
AX = mybir.AxisListType

B = 2
T = 4096
CD = 4096
NH = 32
NKV = 16
HD = 128
SCALE = 1.0 / math.sqrt(32.0)
E = 64
KTOP = 8
EPS = 1e-5
NTOK = 1024
NDS = 16
NCC = 32
NQC = 8
NEG_BIG = 1.0e30

N_TILES_B = 1152
N_PAIRS_B = 576
N_CHUNKS_B = 64


def _tile_base(hb, qc):
    return 144 * hb + 2 * qc * (qc + 1)


def _pair_base(hb, qc):
    return 72 * hb + qc * (qc + 1)


class Sem:

    def __init__(self, ctx, nc, name):
        self.h = ctx.enter_context(nc.semaphore(name))
        self.cnt = 0

    def inc(self, inst, k=1):
        inst.then_inc(self.h, k)
        self.cnt += k
        return inst

    def wait(self, eng, v):
        if v > 0:
            eng.wait_ge(self.h, v)


def build_program(debug_taps=False):
    nc = bass.Bass("TRN2", target_bir_lowering=False, num_devices=8,
                   detect_race_conditions=False)

    x_sh = nc.dram_tensor("x_sh", [NTOK, CD], F32, kind="ExternalInput").ap()
    wt_in = nc.dram_tensor("wt_in", [NDS, 128, NCC * 128], F32, kind="ExternalInput").ap()
    bias_in = nc.dram_tensor("bias_in", [128, NDS], F32, kind="ExternalInput").ap()
    wrt_in = nc.dram_tensor("wrt_in", [128, NCC * 64], F32, kind="ExternalInput").ap()
    br_in = nc.dram_tensor("br_in", [128, E], F32, kind="ExternalInput").ap()
    eb_in = nc.dram_tensor("eb_in", [E, CD], F32, kind="ExternalInput").ap()
    sel_in = nc.dram_tensor("sel_in", [128, 4], F32, kind="ExternalInput").ap()
    y_sh = nc.dram_tensor("y_sh", [NTOK, CD], F32, kind="ExternalOutput").ap()

    if debug_taps:
        dbg_qkv = nc.dram_tensor("dbg_qkv", [3, 128, T], BF16, kind="ExternalOutput").ap()
        dbg_ag2in = nc.dram_tensor("dbg_ag2in", [8 * T, HD], BF16, kind="ExternalOutput").ap()
        dbg_ag2out = nc.dram_tensor("dbg_ag2out", [T, HD], BF16, kind="ExternalOutput").ap()
        dbg_attnhb = nc.dram_tensor("dbg_attnhb", [128, NQC * 512], BF16, kind="ExternalOutput").ap()
        dbg_rden = nc.dram_tensor("dbg_rden", [128, 32], F32, kind="ExternalOutput").ap()
        dbg_attnf = nc.dram_tensor("dbg_attnf", [128, CD], F32, kind="ExternalOutput").ap()
        dbg_cand = nc.dram_tensor("dbg_cand", [128, CD], BF16, kind="ExternalOutput").ap()
        dbg_lg = nc.dram_tensor("dbg_lg", [128, E], F32, kind="ExternalOutput").ap()
        dbg_wn = nc.dram_tensor("dbg_wn", [128, E], F32, kind="ExternalOutput").ap()
        dbg_hT = nc.dram_tensor("dbg_hT", [128, CD], BF16, kind="ExternalOutput").ap()
        dbg_ex = nc.dram_tensor("dbg_ex", [128, E], F32, kind="ExternalOutput").ap()
        dbg_sacc = nc.dram_tensor("dbg_sacc", [128, 1], F32, kind="ExternalOutput").ap()
        dbg_t8 = nc.dram_tensor("dbg_t8", [128, 1], F32, kind="ExternalOutput").ap()
        dbg_m8 = nc.dram_tensor("dbg_m8", [128, E], F32, kind="ExternalOutput").ap()
        dbg_rS = nc.dram_tensor("dbg_rS", [128, 1], F32, kind="ExternalOutput").ap()
        dbg_wv = nc.dram_tensor("dbg_wv", [128, E], F32, kind="ExternalOutput").ap()
        dbg_tmp = nc.dram_tensor("dbg_tmp", [128, CD], F32, kind="ExternalOutput").ap()
        dbg_x1 = nc.dram_tensor("dbg_x1", [128, CD], F32, kind="ExternalOutput").ap()
        dbg_mv = nc.dram_tensor("dbg_mv", [128, 2], F32, kind="ExternalOutput").ap()
        dbg_rstd = nc.dram_tensor("dbg_rstd", [128, 1], F32, kind="ExternalOutput").ap()

    ag1_in = nc.dram_tensor("ag1_in", [CD, NTOK], BF16, kind="Internal").ap()
    ag1_out = nc.dram_tensor("ag1_out", [4 * CD, NTOK], BF16, kind="Internal").ap()
    qkv = nc.dram_tensor("qkv", [NDS, 128, T], BF16, kind="Internal").ap()
    ag2_in = nc.dram_tensor("ag2_in", [8 * T, HD], BF16, kind="Internal").ap()
    bar_in = nc.dram_tensor("bar_in", [1, 8], F32, kind="Internal").ap()
    bar_out = nc.dram_tensor("bar_out", [1, 8], F32, kind="Internal").ap()
    ag2_out = nc.dram_tensor("ag2_out", [4 * 8 * T, HD], BF16, kind="Internal").ap()

    ctx = ExitStack()
    with ctx:
        def sbuf(name, shape, dtype=F32):
            return ctx.enter_context(nc.sbuf_tensor(name, shape, dtype))

        idn = sbuf("idn", [128, 128], BF16)
        ones1 = sbuf("ones1", [128, 1], BF16)
        onesr = sbuf("onesr", [1, 128], BF16)
        ones8 = sbuf("ones8", [1, 8], F32)
        maskp = sbuf("maskp", [128, 2 * 1024], BF16)
        bias_sb = sbuf("bias_sb", [128, NDS], F32)
        br_b = sbuf("br_b", [128, E], F32)
        sel_sb = sbuf("sel_sb", [128, 4], F32)
        wr_sb = sbuf("wr_sb", [128, NCC * 64], BF16)
        eb_sb = sbuf("eb_sb", [E, CD], BF16)
        epsc = sbuf("epsc", [128, 2], F32)
        negM = sbuf("negM", [128, E], F32)
        w_ctx = ExitStack()
        w_sb = w_ctx.enter_context(nc.sbuf_tensor("w_sb", [128, 8 * CD], BF16))

        sm = lambda n: Sem(ctx, nc, n)
        s_ini = sm("s_ini")
        s_icast = sm("s_icast")
        s_w = sm("s_w")
        s_xin = sm("s_xin")
        s_mv0 = sm("s_mv0")
        s_rs0 = sm("s_rs0")
        s_xn = sm("s_xn")
        s_tr0 = sm("s_tr0")
        s_ev0 = sm("s_ev0")
        s_ag1 = sm("s_ag1")
        s_cc = sm("s_cc")
        s_xnt = sm("s_xnt")
        s_xblk = sm("s_xblk")
        s_pj = sm("s_pj")
        s_evp = sm("s_evp")
        s_trv = sm("s_trv")
        s_evv = sm("s_evv")
        s_qkv = sm("s_qkv")
        s_a2k = sm("s_a2k")
        s_a2sq = sm("s_a2sq")
        s_a2m = sm("s_a2m")
        s_a2r = sm("s_a2r")
        s_a2t = sm("s_a2t")
        s_negm = sm("s_negm")
        s_kv = sm("s_kv")
        s_q = sm("s_q")
        s_hfree = sm("s_hfree")
        s_st = sm("s_st")
        s_exp = sm("s_exp")
        s_msk = sm("s_msk")
        s_pv = sm("s_pv")
        s_oev = sm("s_oev")
        s_dcp = sm("s_dcp")
        s_dh = sm("s_dh")
        s_dtr = sm("s_dtr")
        s_rd = sm("s_rd")
        s_atr = sm("s_atr")
        s_est = sm("s_est")
        s_ag2 = sm("s_ag2")
        s_xc = sm("s_xc")
        s_cand = sm("s_cand")
        s_cand2 = sm("s_cand2")
        s_dbgf = sm("s_dbgf")
        s_dbg2 = sm("s_dbg2")
        s_sel = sm("s_sel")
        s_mv1 = sm("s_mv1"); s_rs1 = sm("s_rs1")
        s_x1c = sm("s_x1c")
        s_mv2 = sm("s_mv2"); s_rs2 = sm("s_rs2")
        s_hbf = sm("s_hbf")
        s_htr = sm("s_htr"); s_hev = sm("s_hev")
        s_lgm = sm("s_lgm"); s_lgr = sm("s_lgr")
        s_ex = sm("s_ex"); s_wn = sm("s_wn")
        s_wtr = sm("s_wtr"); s_wte = sm("s_wte")
        s_bm = sm("s_bm"); s_fev = sm("s_fev")
        s_mv3 = sm("s_mv3"); s_rs3 = sm("s_rs3")
        s_fin = sm("s_fin")
        s_out = sm("s_out")

        with ExitStack() as p0:
            sb0 = lambda n, s, d=F32: p0.enter_context(nc.sbuf_tensor(n, s, d))
            ps0 = lambda n, s, d=F32: p0.enter_context(nc.psum_tensor(n, s, d))
            sb_x = [sb0("sb_x0", [128, CD]), sb0("sb_x1", [128, CD])]
            stats0 = sb0("stats0", [128, 48])
            mv0 = [sb0("mv0a", [128, 2]), sb0("mv0b", [128, 2])]
            lntmp0 = sb0("lntmp0", [128, 1])
            rstd0 = [sb0("rstd0a", [128, 1]), sb0("rstd0b", [128, 1])]
            xn_bf = [sb0("xn0", [128, CD], BF16), sb0("xn1", [128, CD], BF16)]
            xnT_loc = sb0("xnT_loc", [128, NCC * NTOK], BF16)
            ps_tr0 = [ps0("ps_tr0a", [128, 1024], BF16), ps0("ps_tr0b", [128, 1024], BF16)]
            blk = p0.enter_context(nc.Block("p0", no_gpsimd_drain=True))

            @blk.sync
            def _(sync):
                s_ini.inc(sync.dma_start(bias_sb[:], bias_in[:]), 16)
                s_ini.inc(sync.dma_start(br_b[:], br_in[:]), 16)
                s_ini.inc(sync.dma_start(sel_sb[:], sel_in[:]), 16)
                for i in range(8):
                    if i >= 2:
                        s_xn.wait(sync, i - 1)
                    s_xin.inc(sync.dma_start(sb_x[i % 2][:], x_sh[128 * i:128 * (i + 1), :]), 16)
                s_ev0.wait(sync, 32)
                for cc in range(NCC):
                    s_ag1.inc(sync.dma_start(
                        ag1_in[128 * cc:128 * (cc + 1), :],
                        xnT_loc[:, 1024 * cc:1024 * (cc + 1)]), 16)

            @blk.gpsimd
            def _(g):
                g.memset(idn[:], 1.0)
                g.affine_select(out=idn[:], in_=idn[:], compare_op=ALU.is_equal,
                                fill=0.0, base=0, pattern=[[-1, 128]], channel_multiplier=1)
                g.memset(ones1[:], 1.0)
                g.memset(onesr[:], 1.0)
                g.memset(ones8[:], 1.0)
                g.memset(epsc[:, 0:1], EPS)
                g.memset(epsc[:, 1:2], 1e-30)
                g.memset(maskp[:], 1.0)
                for u in range(4):
                    g.affine_select(
                        out=maskp[:, 1024 * (u // 2) + 512 * (u % 2):1024 * (u // 2) + 512 * (u % 2) + 512],
                        in_=maskp[:, 1024 * (u // 2) + 512 * (u % 2):1024 * (u // 2) + 512 * (u % 2) + 512],
                        compare_op=ALU.is_ge, fill=0.0, base=-128 * u,
                        pattern=[[1, 512]], channel_multiplier=-1)
                s_ini.inc(g.engine_nop(), 1)
                s_icast.inc(g.dma_start(wr_sb[:], wrt_in[:]), 16)
                s_icast.inc(g.dma_start(eb_sb[:], eb_in[:]), 16)
                for ds in range(8):
                    s_w.inc(g.dma_start(w_sb[:, CD * ds:CD * (ds + 1)],
                                        wt_in[ds]), 16)
                s_ag1.wait(g, 512)
                for k in range(16):
                    s_cc.inc(g.collective_compute(
                        "AllGather", ALU.bypass,
                        replica_groups=[[0, 1, 2, 3], [4, 5, 6, 7]],
                        ins=[ag1_in[256 * k:256 * (k + 1), :].opt()],
                        outs=[ag1_out[1024 * k:1024 * (k + 1), :].opt()]), 1)

            @blk.vector
            def _(v):
                s_ini.wait(v, 49)
                for i in range(8):
                    s_xin.wait(v, 16 * (i + 1))
                    for j in range(8):
                        v.bn_stats(stats0[:, 6 * j:6 * (j + 1)],
                                   sb_x[i % 2][:, 512 * j:512 * (j + 1)])
                    v.bn_stats(stats0[:, 42:48], sb_x[i % 2][:, 3584:4096])
                    v.bn_aggr(mv0[i % 2][:, 0:2], stats0[:, 0:48])
                    s_mv0.inc(v.bn_aggr(mv0[i % 2][:, 0:2], stats0[:, 0:48]), 1)
                    s_rs0.wait(v, i + 1)
                    v.tensor_scalar(
                        xn_bf[i % 2][:], sb_x[i % 2][:],
                        mv0[i % 2][:, 0:1], rstd0[i % 2][:, 0:1],
                        ALU.subtract, ALU.mult)
                    s_xn.inc(v.tensor_scalar(
                        xn_bf[i % 2][:, CD - 256:CD], sb_x[i % 2][:, CD - 256:CD],
                        mv0[i % 2][:, 0:1], rstd0[i % 2][:, 0:1],
                        ALU.subtract, ALU.mult), 1)

            @blk.scalar
            def _(s):
                for i in range(8):
                    s_mv0.wait(s, i + 1)
                    s.activation(lntmp0[:], mv0[i % 2][:, 1:2], AF.Ln, bias=epsc[:, 0:1])
                    s.activation(rstd0[i % 2][:], lntmp0[:], AF.Exp, scale=-0.5)
                    s_rs0.inc(s.activation(rstd0[i % 2][:], lntmp0[:], AF.Exp, scale=-0.5), 1)
                    for grp in range(4):
                        gi = 4 * i + grp
                        s_tr0.wait(s, gi + 1)
                        src = ps_tr0[grp % 2].rearrange("p (j d) -> p j d", d=128)
                        dst = xnT_loc.rearrange("p (c t) -> p c t", t=NTOK)[
                            :, 8 * grp:8 * grp + 8, 128 * i:128 * i + 128]
                        s_ev0.inc(s.activation(dst, src, AF.Copy), 1)

            @blk.tensor
            def _(t):
                for i in range(8):
                    s_xn.wait(t, i + 1)
                    for grp in range(4):
                        gi = 4 * i + grp
                        if gi >= 2:
                            s_ev0.wait(t, gi - 1)
                        for j in list(range(8)) + [7]:
                            cc = 8 * grp + j
                            last = t.transpose(
                                ps_tr0[grp % 2][:, 128 * j:128 * (j + 1)],
                                xn_bf[i % 2][:, 128 * cc:128 * (cc + 1)], idn[:])
                        s_tr0.inc(last, 1)

        assert s_xin.cnt == 128 and s_ag1.cnt == 512 and s_ev0.cnt == 32

        with ExitStack() as p1:
            sb1 = lambda n, s, d=F32: p1.enter_context(nc.sbuf_tensor(n, s, d))
            ps1 = lambda n, s, d=F32: p1.enter_context(nc.psum_tensor(n, s, d))
            xnT_blk = [sb1("xnTb0", [128, NCC * 512], BF16), sb1("xnTb1", [128, NCC * 512], BF16)]
            stg = [sb1(f"stg{k}", [128, 512], BF16) for k in range(4)]
            stgV = [sb1(f"stgV{k}", [128, 512], BF16) for k in range(2)]
            stg2 = [sb1(f"stg2{k}", [128, 512], BF16) for k in range(2)]
            ps_proj = [ps1("ps_pj0", [128, 512]), ps1("ps_pj1", [128, 512])]
            ps_trv = [ps1("ps_trv0", [128, 512], BF16), ps1("ps_trv1", [128, 512], BF16)]
            blk = p1.enter_context(nc.Block("p1", no_gpsimd_drain=True))

            def v_of(blkid, ds):
                return (blkid - 8) * 4 + (ds - 4)

            vidx_to_idx = {}
            for blkid in range(8, 16):
                for ds in range(4, 8):
                    vidx_to_idx[v_of(blkid, ds)] = blkid * 8 + ds

            @blk.sync
            def _(sync):
                s_cc.wait(sync, 16)

                def load_block(blkid):
                    tb = blkid % 8
                    for cc in range(NCC):
                        base = 1024 * (cc // 2) + 256 * (tb // 2) + 128 * (cc % 2)
                        s_xnt.inc(sync.dma_start(
                            xnT_blk[blkid % 2][:, 512 * cc:512 * (cc + 1)],
                            ag1_out[base:base + 128,
                                    512 * (tb % 2):512 * (tb % 2) + 512]), 16)

                load_block(0)
                load_block(1)
                for blkid in range(16):
                    for ds in range(8):
                        gds = (blkid // 8) * 8 + ds
                        idx = blkid * 8 + ds
                        if gds < 12:
                            s_evp.wait(sync, idx + 1)
                            src = stg[idx % 4]
                        else:
                            vi = v_of(blkid, ds)
                            s_evv.wait(sync, vi + 1)
                            src = stg2[vi % 2]
                        tb = blkid % 8
                        s_qkv.inc(sync.dma_start(
                            qkv[gds][:, 512 * tb:512 * (tb + 1)], src[:]), 16)
                    if blkid + 2 <= 15:
                        s_evp.wait(sync, 8 * (blkid + 1))
                        load_block(blkid + 2)

            @blk.gpsimd
            def _(g):
                s_evp.wait(g, 64)
                for ds in range(8):
                    s_w.inc(g.dma_start(w_sb[:, CD * ds:CD * (ds + 1)],
                                        wt_in[8 + ds]), 16)

            @blk.tensor
            def _(t):
                for blkid in range(16):
                    s_xnt.wait(t, 512 * (blkid + 1))
                    for ds in range(8):
                        gds = (blkid // 8) * 8 + ds
                        idx = blkid * 8 + ds
                        s_w.wait(t, 16 * ((blkid // 8) * 8 + ds + 1))
                        if idx >= 2:
                            s_evp.wait(t, idx - 1)
                        for cc in range(NCC):
                            last = t.matmul(
                                ps_proj[idx % 2][:],
                                w_sb[:, CD * ds + 128 * cc:CD * ds + 128 * (cc + 1)],
                                xnT_blk[blkid % 2][:, 512 * cc:512 * (cc + 1)],
                                start=(cc == 0), stop=(cc == NCC - 1))
                        s_pj.inc(last, 1)
                    if blkid >= 8:
                        for ds in range(4, 8):
                            idx = blkid * 8 + ds
                            vi = v_of(blkid, ds)
                            s_evp.wait(t, idx + 1)
                            s_evv.wait(t, vi - 1)
                            for sb_i in [0, 1, 2, 3, 3]:
                                last = t.transpose(
                                    ps_trv[vi % 2][:, 128 * sb_i:128 * (sb_i + 1)],
                                    stgV[vi % 2][:, 128 * sb_i:128 * (sb_i + 1)],
                                    idn[:])
                            s_trv.inc(last, 1)

            @blk.scalar
            def _(s):
                for blkid in range(16):
                    for ds in range(8):
                        gds = (blkid // 8) * 8 + ds
                        idx = blkid * 8 + ds
                        s_pj.wait(s, idx + 1)
                        if gds < 12:
                            if idx >= 4:
                                s_qkv.wait(s, 16 * (idx - 3))
                            s_evp.inc(s.add(stg[idx % 4][:], ps_proj[idx % 2][:],
                                            bias_sb[:, gds:gds + 1]), 1)
                        else:
                            vi = v_of(blkid, ds)
                            if vi >= 2:
                                s_trv.wait(s, vi - 1)
                            s_evp.inc(s.add(stgV[vi % 2][:], ps_proj[idx % 2][:],
                                            bias_sb[:, gds:gds + 1]), 1)
                    if blkid >= 8:
                        for ds in range(4, 8):
                            vi = v_of(blkid, ds)
                            s_trv.wait(s, vi + 1)
                            if vi >= 2:
                                s_qkv.wait(s, 16 * (vidx_to_idx[vi - 2] + 1))
                            s_evv.inc(s.activation(
                                stg2[vi % 2][:], ps_trv[vi % 2][:], AF.Copy), 1)

        assert s_qkv.cnt == 16 * 128 and s_pj.cnt == 128
        assert s_evv.cnt == 32 and s_trv.cnt == 32
        w_ctx.close()

        with ExitStack() as p2:
            sb2 = lambda n, s, d=F32: p2.enter_context(nc.sbuf_tensor(n, s, d))
            ps2_ = lambda n, s, d=F32: p2.enter_context(nc.psum_tensor(n, s, d))
            a2buf = [sb2("a2b0", [128, T], BF16), sb2("a2b1", [128, T], BF16)]
            sqbuf = [sb2("a2s0", [128, T], BF16), sb2("a2s1", [128, T], BF16)]
            kmax2 = sb2("kmax2", [1, 4])
            km8 = sb2("km8", [1, 8])
            qm2 = sb2("qm2", [1, E])
            k2row = sb2("k2row", [1, E])
            m2row = sb2("m2row", [1, E])
            mrow = sb2("mrow", [1, E])
            mrow2 = sb2("mrow2", [1, E])
            negrow = sb2("negrow", [1, E])
            negbf = sb2("negbf", [1, E], BF16)
            tmp11 = sb2("tmp11", [1, 1])
            ps_nrm = ps2_("ps_nrm", [1, 512])
            ps_bc = ps2_("ps_bc", [128, E])
            blk = p2.enter_context(nc.Block("p2", no_gpsimd_drain=True))

            @blk.sync
            def _(sync):
                for j in range(12):
                    if j >= 2:
                        s_a2sq.wait(sync, j - 1)
                    gds = 8 + j if j < 4 else j - 4
                    s_a2k.inc(sync.dma_start(a2buf[j % 2][:], qkv[gds][:, :]), 16)

            @blk.vector
            def _(v):
                for j in range(12):
                    s_a2k.wait(v, 16 * (j + 1))
                    if j >= 2:
                        s_a2m.wait(v, 8 * (j - 1))
                    v.tensor_mul(sqbuf[j % 2][:], a2buf[j % 2][:], a2buf[j % 2][:])
                    s_a2sq.inc(v.tensor_mul(sqbuf[j % 2][:, T - 512:T],
                                            a2buf[j % 2][:, T - 512:T],
                                            a2buf[j % 2][:, T - 512:T]), 1)
                    for c8 in range(8):
                        s_a2m.wait(v, 8 * j + c8 + 1)
                        if j < 4:
                            inst = v.reduce_max(km8[0:1, c8:c8 + 1], ps_nrm[0:1, :], AX.X)
                        else:
                            inst = v.reduce_max(qm2[0:1, 8 * (j - 4) + c8:8 * (j - 4) + c8 + 1],
                                                ps_nrm[0:1, :], AX.X)
                        s_a2r.inc(inst, 1)
                    if j < 4:
                        v.reduce_max(kmax2[0:1, j:j + 1], km8[0:1, 0:8], AX.X)
                        v.reduce_max(kmax2[0:1, j:j + 1], km8[0:1, 0:8], AX.X)
                s_a2r.wait(v, 96)
                for h in range(8):
                    v.tensor_scalar(k2row[0:1, 8 * h:8 * (h + 1)], ones8[0:1, :],
                                    kmax2[0:1, h % 4:h % 4 + 1], None, ALU.mult)
                v.tensor_mul(m2row[0:1, :], qm2[0:1, :], k2row[0:1, :])
                s_a2t.inc(v.tensor_mul(m2row[0:1, :], qm2[0:1, :], k2row[0:1, :]), 1)

            @blk.scalar
            def _(s):
                s_a2t.wait(s, 1)
                s.activation(mrow2[0:1, :], m2row[0:1, :], AF.Ln, bias=epsc[0:1, 1:2])
                s_a2t.inc(s.activation(mrow[0:1, :], mrow2[0:1, :], AF.Exp, scale=0.5), 1)

            @blk.vector
            def _(v):
                s_a2t.wait(v, 2)
                v.tensor_scalar(negrow[0:1, :], mrow[0:1, :], -SCALE, None, ALU.mult)
                v.tensor_copy(negbf[0:1, :], negrow[0:1, :])
                s_a2t.inc(v.tensor_copy(negbf[0:1, :], negrow[0:1, :]), 1)

            @blk.tensor
            def _(t):
                for j in range(12):
                    s_a2sq.wait(t, j + 1)
                    for c8 in range(8):
                        if 8 * j + c8 >= 1:
                            s_a2r.wait(t, 8 * j + c8)
                        s_a2m.inc(t.matmul(ps_nrm[0:1, :], ones1[:],
                                           sqbuf[j % 2][:, 512 * c8:512 * (c8 + 1)],
                                           start=True, stop=True), 1)
                s_a2t.wait(t, 3)
                s_a2m.inc(t.matmul(ps_bc[:], onesr[0:1, :], negbf[0:1, :],
                                   start=True, stop=True), 1)

            @blk.vector
            def _(v):
                s_a2m.wait(v, 97)
                v.tensor_copy(negM[:], ps_bc[:])
                s_negm.inc(v.tensor_copy(negM[:], ps_bc[:]), 1)

        with ExitStack() as p3:
            sb3 = lambda n, s, d=F32: p3.enter_context(nc.sbuf_tensor(n, s, d))
            ps3 = lambda n, s, d=F32: p3.enter_context(nc.psum_tensor(n, s, d))
            q_sb = [sb3("q_sb0", [128, T], BF16), sb3("q_sb1", [128, T], BF16)]
            k_sb = [sb3("k_sb0", [128, T], BF16), sb3("k_sb1", [128, T], BF16)]
            v_sb = [sb3("v_sb0", [128, T], BF16), sb3("v_sb1", [128, T], BF16)]
            pt = [sb3("pt0", [128, 1024], BF16), sb3("pt1", [128, 1024], BF16)]
            ptm = [sb3("ptm0", [128, 1024], BF16), sb3("ptm1", [128, 1024], BF16)]
            attn_hb = [sb3("attn0", [128, NQC * 512], BF16), sb3("attn1", [128, NQC * 512], BF16)]
            den_stage = [sb3("denst0", [1, 512]), sb3("denst1", [1, 512])]
            den_hb = [sb3("denh0", [128, 512], BF16), sb3("denh1", [128, 512], BF16)]
            rden = [sb3("rden0", [128, 32]), sb3("rden1", [128, 32])]
            rden_raw = sb3("rden_raw", [128, 32])
            estg = [sb3("estg0", [128, 512], BF16), sb3("estg1", [128, 512], BF16)]
            ps_st = [ps3("ps_st0", [128, 1024]), ps3("ps_st1", [128, 1024])]
            ps_out = [ps3("ps_out0", [128, 512]), ps3("ps_out1", [128, 512])]
            ps_den = ps3("ps_den", [1, 512])
            ps_ep = ps3("ps_ep", [128, 512], BF16)
            blk = p3.enter_context(nc.Block("p3", no_gpsimd_drain=True))

            def gds_of(hb):
                return (hb % 2) * 4 + hb // 2

            @blk.sync
            def _(sync):
                def loads(h):
                    kvl, g = h // 2, h % 2
                    if g == 0:
                        if kvl >= 2:
                            s_oev.wait(sync, 16 * kvl - 16)
                        s_kv.inc(sync.dma_start(k_sb[kvl % 2][:], qkv[8 + kvl][:, :]), 16)
                        s_kv.inc(sync.dma_start(v_sb[kvl % 2][:], qkv[12 + kvl][:, :]), 16)
                    if h >= 2:
                        s_oev.wait(sync, 8 * (h - 1))
                    s_q.inc(sync.dma_start(q_sb[h % 2][:], qkv[gds_of(h)][:, :]), 16)

                loads(0)
                loads(1)
                for hb in range(8):
                    if hb + 2 <= 7:
                        loads(hb + 2)
                    for qc in range(NQC):
                        gc = 8 * hb + qc
                        s_est.wait(sync, gc + 1)
                        dst = ag2_in.rearrange("(a p) d -> p a d", p=128)[
                            :, 32 * hb + 4 * qc: 32 * hb + 4 * qc + 4, :]
                        s_ag2.inc(sync.dma_start(dst, estg[gc % 2].rearrange(
                            "p (s d) -> p s d", d=128)), 16)
                if debug_taps:
                    s_est.wait(sync, 64)
                    s_ag2.inc(sync.dma_start(dbg_attnhb[:], attn_hb[1][:]), 16)
                    s_ag2.inc(sync.dma_start(dbg_rden[:], rden[1][:]), 16)

            @blk.gpsimd
            def _(g):
                for gc in range(N_CHUNKS_B):
                    s_dcp.wait(g, gc + 1)
                    hb, qc = gc // 8, gc % 8
                    s_dh.inc(g.dma_start(den_hb[hb % 2][qc:qc + 1, :],
                                         den_stage[gc % 2][0:1, :]), 16)
                s_ag2.wait(g, 16 * (N_CHUNKS_B + 2) if debug_taps else 16 * N_CHUNKS_B)
                for k in range(16):
                    s_cc.inc(g.collective_compute(
                        "AllGather", ALU.bypass,
                        replica_groups=[[0, 1, 2, 3], [4, 5, 6, 7]],
                        ins=[ag2_in[2048 * k:2048 * (k + 1), :].opt()],
                        outs=[ag2_out[8192 * k:8192 * (k + 1), :].opt()]), 1)

            @blk.tensor
            def _(t):
                s_negm.wait(t, 1)

                def pvden(hb, qc, q):
                    kvl = hb // 2
                    gp = _pair_base(hb, qc) + q
                    tb_ = _tile_base(hb, qc)
                    nt = 4 * qc + 4
                    np_ = 2 * qc + 2
                    gc = 8 * hb + qc
                    if q == 0:
                        if gc >= 2:
                            s_oev.wait(t, gc - 1)
                        if gc >= 1:
                            s_dcp.wait(t, gc)
                    s_exp.wait(t, gp + 1)
                    diag = q >= np_ - 2
                    if diag:
                        d_no = 2 * gc + (q - (np_ - 2)) + 1
                        s_msk.wait(t, d_no)
                    pbuf = ptm[gp % 2] if diag else pt[gp % 2]
                    for half in range(2):
                        tt = 2 * q + half
                        t.matmul(ps_out[qc % 2][:],
                                 v_sb[kvl % 2][:, 128 * tt:128 * (tt + 1)],
                                 pbuf[:, 512 * half:512 * (half + 1)],
                                 start=(tt == 0), stop=(tt == nt - 1))
                        last = t.matmul(ps_den[0:1, :], ones1[:],
                                        pbuf[:, 512 * half:512 * (half + 1)],
                                        start=(tt == 0), stop=(tt == nt - 1))
                        s_pv.inc(last, 1)

                def epilogue_piece(hb, qc):
                    gc = 8 * hb + qc
                    if qc == 0:
                        s_dh.wait(t, 16 * (8 * hb + 8))
                        if hb >= 1:
                            s_rd.wait(t, hb)
                            s_est.wait(t, 8 * hb)
                        for tt in [0, 1, 2, 3, 3]:
                            last = t.transpose(ps_ep[:, 128 * tt:128 * (tt + 1)],
                                               den_hb[hb % 2][:, 128 * tt:128 * (tt + 1)],
                                               idn[:])
                        s_dtr.inc(last, 1)
                    s_oev.wait(t, gc + 1)
                    if qc == 0:
                        s_rd.wait(t, hb + 1)
                    else:
                        s_est.wait(t, gc)
                    for sb_i in [0, 1, 2, 3, 3]:
                        last = t.transpose(
                            ps_ep[:, 128 * sb_i:128 * (sb_i + 1)],
                            attn_hb[hb % 2][:, 512 * qc + 128 * sb_i:512 * qc + 128 * (sb_i + 1)],
                            idn[:])
                    s_atr.inc(last, 1)

                for hb in range(8):
                    kvl = hb // 2
                    s_kv.wait(t, 32 * (kvl + 1))
                    s_q.wait(t, 16 * (hb + 1))
                    for qc in range(NQC):
                        if hb >= 1:
                            epilogue_piece(hb - 1, qc)
                        np_ = 2 * qc + 2
                        pbase = _pair_base(hb, qc)
                        tbase = _tile_base(hb, qc)
                        for p in range(np_):
                            gp = pbase + p
                            if gp >= 2:
                                s_exp.wait(t, gp - 1)
                            for half in range(2):
                                tt = 2 * p + half
                                last = t.matmul(
                                    ps_st[gp % 2][:, 512 * half:512 * (half + 1)],
                                    k_sb[kvl % 2][:, 128 * tt:128 * (tt + 1)],
                                    q_sb[hb % 2][:, 512 * qc:512 * (qc + 1)],
                                    start=True, stop=True)
                                s_st.inc(last, 1)
                            if p >= 1:
                                pvden(hb, qc, p - 1)
                        pvden(hb, qc, np_ - 1)
                for qc in range(NQC):
                    epilogue_piece(7, qc)

            @blk.scalar
            def _(s):
                s_negm.wait(s, 1)
                for hb in range(8):
                    gds = gds_of(hb)
                    for qc in range(NQC):
                        gc = 8 * hb + qc
                        np_ = 2 * qc + 2
                        pbase = _pair_base(hb, qc)
                        tbase = _tile_base(hb, qc)
                        for p in range(np_):
                            gp = pbase + p
                            s_st.wait(s, tbase + 2 * p + 2)
                            if gp >= 2:
                                s_pv.wait(s, 2 * gp - 2)
                            s_exp.inc(s.activation(
                                pt[gp % 2][:], ps_st[gp % 2][:, 0:1024], AF.Exp,
                                scale=SCALE,
                                bias=negM[:, 8 * gds + qc:8 * gds + qc + 1]), 1)
                        s_pv.wait(s, tbase + 4 * qc + 4)
                        s_oev.inc(s.activation(
                            attn_hb[hb % 2][:, 512 * qc:512 * (qc + 1)],
                            ps_out[qc % 2][:], AF.Copy), 1)

            @blk.vector
            def _(v):
                for hb in range(8):
                    for qc in range(NQC):
                        gc = 8 * hb + qc
                        np_ = 2 * qc + 2
                        pbase = _pair_base(hb, qc)
                        tbase = _tile_base(hb, qc)
                        if hb >= 1:
                            if qc == 0:
                                s_dtr.wait(v, hb)
                                for tt in range(4):
                                    v.tensor_scalar(rden_raw[:, 8 * tt:8 * (tt + 1)],
                                                    ps_ep[:, 128 * tt:128 * tt + 8],
                                                    1e-37, None, ALU.add)
                                v.reciprocal(rden[(hb - 1) % 2][:, 0:32],
                                             rden_raw[:, 0:32])
                                s_rd.inc(v.reciprocal(rden[(hb - 1) % 2][:, 0:32],
                                                      rden_raw[:, 0:32]), 1)
                            gcp = 8 * (hb - 1) + qc
                            s_atr.wait(v, gcp + 1)
                            if gcp >= 2:
                                s_ag2.wait(v, 16 * (gcp - 1))
                            for sb_i in [0, 1, 2, 3, 3]:
                                last = v.tensor_scalar(
                                    estg[gcp % 2][:, 128 * sb_i:128 * (sb_i + 1)],
                                    ps_ep[:, 128 * sb_i:128 * (sb_i + 1)],
                                    rden[(hb - 1) % 2][:, 8 * sb_i + qc:8 * sb_i + qc + 1],
                                    None, ALU.mult)
                            s_est.inc(last, 1)
                        for dq in range(2):
                            p = np_ - 2 + dq
                            gp = pbase + p
                            s_exp.wait(v, gp + 1)
                            if gp >= 2:
                                s_pv.wait(v, 2 * gp - 2)
                            v.tensor_mul(ptm[gp % 2][:], pt[gp % 2][:],
                                         maskp[:, 1024 * dq:1024 * (dq + 1)])
                            s_msk.inc(v.tensor_mul(
                                ptm[gp % 2][:, 896:1024], pt[gp % 2][:, 896:1024],
                                maskp[:, 1024 * dq + 896:1024 * dq + 1024]), 1)
                        s_pv.wait(v, tbase + 4 * qc + 4)
                        if gc >= 2:
                            s_dh.wait(v, 16 * (gc - 1))
                        v.tensor_copy(den_stage[gc % 2][0:1, :], ps_den[0:1, :])
                        s_dcp.inc(v.tensor_copy(den_stage[gc % 2][0:1, 384:512],
                                                ps_den[0:1, 384:512]), 1)
                for qc in range(NQC):
                    if qc == 0:
                        s_dtr.wait(v, 8)
                        for tt in range(4):
                            v.tensor_scalar(rden_raw[:, 8 * tt:8 * (tt + 1)],
                                            ps_ep[:, 128 * tt:128 * tt + 8],
                                            1e-37, None, ALU.add)
                        v.reciprocal(rden[7 % 2][:, 0:32], rden_raw[:, 0:32])
                        s_rd.inc(v.reciprocal(rden[7 % 2][:, 0:32], rden_raw[:, 0:32]), 1)
                    gcp = 56 + qc
                    s_atr.wait(v, gcp + 1)
                    s_ag2.wait(v, 16 * (gcp - 1))
                    for sb_i in [0, 1, 2, 3, 3]:
                        last = v.tensor_scalar(
                            estg[gcp % 2][:, 128 * sb_i:128 * (sb_i + 1)],
                            ps_ep[:, 128 * sb_i:128 * (sb_i + 1)],
                            rden[7 % 2][:, 8 * sb_i + qc:8 * sb_i + qc + 1],
                            None, ALU.mult)
                    s_est.inc(last, 1)

        assert s_st.cnt == N_TILES_B and s_pv.cnt == N_TILES_B
        assert s_exp.cnt == N_PAIRS_B and s_msk.cnt == 2 * N_CHUNKS_B
        assert s_oev.cnt == N_CHUNKS_B and s_est.cnt == N_CHUNKS_B
        assert s_ag2.cnt == 16 * (N_CHUNKS_B + 2 if debug_taps else N_CHUNKS_B)

        with ExitStack() as p4:
            sb4 = lambda n, s, d=F32: p4.enter_context(nc.sbuf_tensor(n, s, d))
            ps4 = lambda n, s, d=F32: p4.enter_context(nc.psum_tensor(n, s, d))
            x_sb = [sb4("x_sb0", [128, CD])]
            cand = [sb4(f"cand{k}", [128, CD], BF16) for k in range(4)]
            attn_f = [sb4("attnf0", [128, CD]), sb4("attnf1", [128, CD])]
            stats4 = sb4("stats4", [128, 48])
            mvx = [sb4("mvx0", [128, 2]), sb4("mvx1", [128, 2])]
            rstdx = [sb4("rstdx0", [128, 1]), sb4("rstdx1", [128, 1])]
            lntmp4 = sb4("lntmp4", [128, 1])
            tmp_t = sb4("tmp_t", [128, CD])
            x1_sb = [sb4("x1_0", [128, CD])]
            h_f = sb4("h_f", [128, CD])
            h_bf = sb4("h_bf", [128, CD], BF16)
            hT = [sb4("hT0", [128, CD], BF16)]
            lg = sb4("lg", [128, E])
            negmx = sb4("negmx", [128, 1])
            ex = sb4("ex", [128, E])
            sacc = sb4("sacc", [128, 1])
            rwork = sb4("rwork", [128, E])
            rwork2 = sb4("rwork2", [128, E])
            m8 = sb4("m8", [128, 1])
            mask8 = sb4("mask8", [128, E])
            t8 = sb4("t8", [128, 1])
            wv = sb4("wv", [128, E])
            rS = sb4("rS", [128, 1])
            wn = sb4("wn", [128, E])
            wsum = sb4("wsum", [128, 1])
            wn_bf = sb4("wn_bf", [128, E], BF16)
            wnT = sb4("wnT", [64, 128], BF16)
            ffn = sb4("ffn", [128, CD])
            out_sb = [sb4("out0", [128, CD])]
            ps_htr = [ps4("ps_htr0", [128, 1024], BF16), ps4("ps_htr1", [128, 1024], BF16)]
            ps_log = ps4("ps_log", [128, E])
            ps_wtr = ps4("ps_wtr", [64, 128], BF16)
            ps_bm = [ps4("ps_bm0", [128, 512]), ps4("ps_bm1", [128, 512])]
            blk = p4.enter_context(nc.Block("p4", no_gpsimd_drain=True))

            @blk.sync
            def _(sync):
                s_cc.wait(sync, 32)
                s_ini.wait(sync, 49)
                for i in range(8):
                    if i == 1 and debug_taps:
                        s_x1c.wait(sync, 1)
                        s_dbg2.inc(sync.dma_start(dbg_tmp[:], tmp_t[:]), 16)
                        s_dbg2.inc(sync.dma_start(dbg_x1[:], x1_sb[0][:]), 16)
                        s_dbg2.inc(sync.dma_start(dbg_mv[:], mvx[0][:, 0:2]), 16)
                        s_dbg2.inc(sync.dma_start(dbg_rstd[:], rstdx[0][:]), 16)
                        s_dbgf.inc(sync.dma_start(dbg_cand[:], cand[0][:]), 16)
                        s_sel.wait(sync, 1)
                        s_dbgf.inc(sync.dma_start(dbg_attnf[:], attn_f[1][:]), 16)
                        s_lgr.wait(sync, 1)
                        s_dbgf.inc(sync.dma_start(dbg_lg[:], lg[:]), 16)
                        s_wn.wait(sync, 1)
                        s_dbgf.inc(sync.dma_start(dbg_wn[:], wn[:]), 16)
                        s_hbf.wait(sync, 1)
                        s_dbgf.inc(sync.dma_start(dbg_hT[:], hT[0][:]), 16)
                        s_dbgf.inc(sync.dma_start(dbg_ex[:], ex[:]), 16)
                        s_dbgf.inc(sync.dma_start(dbg_sacc[:], sacc[:]), 16)
                        s_dbgf.inc(sync.dma_start(dbg_t8[:], t8[:]), 16)
                        s_dbgf.inc(sync.dma_start(dbg_m8[:], mask8[:]), 16)
                        s_dbgf.inc(sync.dma_start(dbg_rS[:], rS[:]), 16)
                        s_dbgf.inc(sync.dma_start(dbg_wv[:], wv[:]), 16)
                    if i >= 1:
                        s_x1c.wait(sync, i)
                    s_xc.inc(sync.dma_start(x_sb[0][:], x_sh[128 * i:128 * (i + 1), :]), 16)
                    if i >= 1:
                        s_sel.wait(sync, i)
                    for rep in range(2 if i == 0 else 1):
                        for cnd in range(4):
                            for R in range(4):
                                tok = 1024 * cnd + 128 * i
                                hview = ag2_out.rearrange(
                                    "(h s r t) d -> h s r t d", h=8, s=2, r=4)[
                                    :, tok // 2048, R,
                                    tok % 2048: tok % 2048 + 128, :].transpose([1, 0, 2])
                                dst = cand[cnd].rearrange("p (r h d) -> p r h d", r=4, h=8)[
                                    :, R, :, :]
                                sem = s_cand2 if rep == 1 else s_cand
                                sem.inc(sync.dma_start(dst, hview), 16)
                    if i >= 1:
                        s_fin.wait(sync, i)
                        s_out.inc(sync.dma_start(y_sh[128 * (i - 1):128 * i, :],
                                                 out_sb[0][:]), 16)
                s_fin.wait(sync, 8)
                s_out.inc(sync.dma_start(y_sh[128 * 7:128 * 8, :], out_sb[0][:]), 16)
                if debug_taps:
                    for j, gds in enumerate([0, 8, 12]):
                        s_out.inc(sync.dma_start(dbg_qkv[j], qkv[gds][:, :]), 16)
                    s_out.inc(sync.dma_start(dbg_ag2in[:], ag2_in[:]), 16)
                    s_out.inc(sync.dma_start(dbg_ag2out[:], ag2_out[0:T, :]), 16)
                    sync.wait_ge(s_out.h, 128 + 5 * 16)
                else:
                    sync.wait_ge(s_out.h, 128)

            @blk.vector
            def _(v):
                s_ini.wait(v, 49)
                for i in range(8):
                    s_cand.wait(v, 256 * (i + 1))
                    if i == 0:
                        s_cand2.wait(v, 256)
                    if i == 1 and debug_taps:
                        s_dbgf.wait(v, 176)
                    v.tensor_scalar(attn_f[0][:], cand[0][:],
                                    sel_sb[:, 0:1], None, ALU.mult)
                    for cnd in range(1, 4):
                        last = v.scalar_tensor_tensor(
                            attn_f[cnd % 2][:], cand[cnd][:], sel_sb[:, cnd:cnd + 1],
                            attn_f[(cnd + 1) % 2][:], ALU.mult, ALU.add)
                    s_sel.inc(last, 1)
                    for j in range(8):
                        v.bn_stats(stats4[:, 6 * j:6 * (j + 1)],
                                   attn_f[1][:, 512 * j:512 * (j + 1)])
                    v.bn_stats(stats4[:, 42:48], attn_f[1][:, 3584:4096])
                    v.bn_aggr(mvx[0][:, 0:2], stats4[:, 0:48])
                    s_mv1.inc(v.bn_aggr(mvx[0][:, 0:2], stats4[:, 0:48]), 1)
                    s_rs1.wait(v, i + 1)
                    v.tensor_scalar(tmp_t[:], attn_f[1][:], mvx[0][:, 0:1],
                                    rstdx[0][:, 0:1], ALU.subtract, ALU.mult)
                    s_xc.wait(v, 16 * (i + 1))
                    s_x1c.inc(v.tensor_add(x1_sb[0][:], tmp_t[:], x_sb[0][:]), 1)
                    if i == 0 and debug_taps:
                        s_dbg2.wait(v, 64)
                    for j in range(8):
                        v.bn_stats(stats4[:, 6 * j:6 * (j + 1)],
                                   x1_sb[0][:, 512 * j:512 * (j + 1)])
                    v.bn_stats(stats4[:, 42:48], x1_sb[0][:, 3584:4096])
                    v.bn_aggr(mvx[1][:, 0:2], stats4[:, 0:48])
                    s_mv2.inc(v.bn_aggr(mvx[1][:, 0:2], stats4[:, 0:48]), 1)
                    s_rs2.wait(v, i + 1)
                    v.tensor_scalar(h_f[:], x1_sb[0][:], mvx[1][:, 0:1],
                                    rstdx[1][:, 0:1], ALU.subtract, ALU.mult)
                    v.tensor_copy(h_bf[:], h_f[:])
                    s_hbf.inc(v.tensor_copy(h_bf[:, CD - 256:CD], h_f[:, CD - 256:CD]), 1)
                    s_lgm.wait(v, i + 1)
                    v.tensor_add(lg[:], ps_log[:], br_b[:])
                    v.reduce_max(negmx[:, 0:1], lg[:], AX.X, negate=True)
                    s_lgr.inc(v.reduce_max(negmx[:, 0:1], lg[:], AX.X, negate=True), 1)
                    s_ex.wait(v, i + 1)
                    v.tensor_copy(rwork[:], ex[:])
                    rw = [rwork, rwork2]
                    for it in range(KTOP - 1):
                        v.reduce_max(m8[:, 0:1], rw[it % 2][:], AX.X)
                        v.reduce_max(m8[:, 0:1], rw[it % 2][:], AX.X)
                        v.tensor_scalar(mask8[:], rw[it % 2][:], m8[:, 0:1], None, ALU.is_ge)
                        v.scalar_tensor_tensor(rw[(it + 1) % 2][:], mask8[:], -NEG_BIG,
                                               rw[it % 2][:], ALU.mult, ALU.add)
                    v.reduce_max(t8[:, 0:1], rw[(KTOP - 1) % 2][:], AX.X)
                    v.reduce_max(t8[:, 0:1], rw[(KTOP - 1) % 2][:], AX.X)
                    v.tensor_scalar(mask8[:], ex[:], t8[:, 0:1], None, ALU.is_ge)
                    v.tensor_mul(wv[:], ex[:], mask8[:])
                    v.reciprocal(rS[:, 0:1], sacc[:, 0:1])
                    v.reciprocal(rS[:, 0:1], sacc[:, 0:1])
                    v.tensor_scalar(wn[:], wv[:], rS[:, 0:1], None, ALU.mult)
                    v.reduce_sum(wsum[:, 0:1], wn[:], AX.X)
                    v.reduce_sum(wsum[:, 0:1], wn[:], AX.X)
                    v.tensor_copy(wn_bf[:], wn[:])
                    s_wn.inc(v.tensor_copy(wn_bf[:], wn[:]), 1)
                    for fb in range(8):
                        s_bm.wait(v, 8 * i + fb + 1)
                        s_fev.inc(v.scalar_tensor_tensor(
                            ffn[:, 512 * fb:512 * (fb + 1)],
                            h_f[:, 512 * fb:512 * (fb + 1)], wsum[:, 0:1],
                            ps_bm[fb % 2][:], ALU.mult, ALU.add), 1)
                    for j in range(8):
                        v.bn_stats(stats4[:, 6 * j:6 * (j + 1)],
                                   ffn[:, 512 * j:512 * (j + 1)])
                    v.bn_stats(stats4[:, 42:48], ffn[:, 3584:4096])
                    v.bn_aggr(mvx[0][:, 0:2], stats4[:, 0:48])
                    s_mv3.inc(v.bn_aggr(mvx[0][:, 0:2], stats4[:, 0:48]), 1)
                    s_rs3.wait(v, i + 1)
                    v.tensor_scalar(tmp_t[:], ffn[:], mvx[0][:, 0:1],
                                    rstdx[0][:, 0:1], ALU.subtract, ALU.mult)
                    s_out.wait(v, 16 * i)
                    v.tensor_add(out_sb[0][:], tmp_t[:], x1_sb[0][:])
                    s_fin.inc(v.tensor_add(out_sb[0][:, CD - 256:CD],
                                           tmp_t[:, CD - 256:CD],
                                           x1_sb[0][:, CD - 256:CD]), 1)

            @blk.scalar
            def _(s):
                for i in range(8):
                    s_mv1.wait(s, i + 1)
                    s.activation(lntmp4[:], mvx[0][:, 1:2], AF.Ln, bias=epsc[:, 0:1])
                    s.activation(rstdx[0][:], lntmp4[:], AF.Exp, scale=-0.5)
                    s_rs1.inc(s.activation(rstdx[0][:], lntmp4[:], AF.Exp, scale=-0.5), 1)
                    s_mv2.wait(s, i + 1)
                    s.activation(lntmp4[:], mvx[1][:, 1:2], AF.Ln, bias=epsc[:, 0:1])
                    s.activation(rstdx[1][:], lntmp4[:], AF.Exp, scale=-0.5)
                    s_rs2.inc(s.activation(rstdx[1][:], lntmp4[:], AF.Exp, scale=-0.5), 1)
                    if i == 1 and debug_taps:
                        s_dbgf.wait(s, 176)
                    s_lgm.wait(s, i)
                    for grp in range(4):
                        gi = 4 * i + grp
                        s_htr.wait(s, gi + 1)
                        s_hev.inc(s.activation(hT[0][:, 1024 * grp:1024 * (grp + 1)],
                                               ps_htr[grp % 2][:, 0:1024], AF.Copy), 1)
                    s_lgr.wait(s, i + 1)
                    s.activation(ex[:], lg[:], AF.Exp, bias=negmx[:, 0:1],
                                 accum_out=sacc[:, 0:1])
                    s_ex.inc(s.activation(ex[:, 48:64], lg[:, 48:64], AF.Exp,
                                          bias=negmx[:, 0:1]), 1)
                    s_wtr.wait(s, i + 1)
                    s.activation(wnT[0:64, :], ps_wtr[0:64, :], AF.Copy)
                    s_wte.inc(s.activation(wnT[0:64, :], ps_wtr[0:64, :], AF.Copy), 1)
                    s_mv3.wait(s, i + 1)
                    s.activation(lntmp4[:], mvx[0][:, 1:2], AF.Ln, bias=epsc[:, 0:1])
                    s.activation(rstdx[0][:], lntmp4[:], AF.Exp, scale=-0.5)
                    s_rs3.inc(s.activation(rstdx[0][:], lntmp4[:], AF.Exp, scale=-0.5), 1)

            @blk.tensor
            def _(t):
                s_icast.wait(t, 32)
                for i in range(8):
                    s_hbf.wait(t, i + 1)
                    for grp in range(4):
                        gi = 4 * i + grp
                        if gi >= 2:
                            s_hev.wait(t, gi - 1)
                        for j in list(range(8)) + [7]:
                            cc = 8 * grp + j
                            last = t.transpose(ps_htr[grp % 2][:, 128 * j:128 * (j + 1)],
                                               h_bf[:, 128 * cc:128 * (cc + 1)], idn[:])
                        s_htr.inc(last, 1)
                    s_hev.wait(t, 4 * (i + 1))
                    if i >= 1:
                        s_lgr.wait(t, i)
                    for cc in range(NCC):
                        last = t.matmul(ps_log[:], hT[0][:, 128 * cc:128 * (cc + 1)],
                                        wr_sb[:, 64 * cc:64 * (cc + 1)],
                                        start=(cc == 0), stop=(cc == NCC - 1))
                    s_lgm.inc(last, 1)
                    s_wn.wait(t, i + 1)
                    if i >= 1:
                        s_wte.wait(t, i)
                    t.transpose(ps_wtr[0:64, 0:128], wn_bf[:, 0:64], idn[:])
                    s_wtr.inc(t.transpose(ps_wtr[0:64, 0:128], wn_bf[:, 0:64], idn[:]), 1)
                    s_wte.wait(t, i + 1)
                    for fb in range(8):
                        bmi = 8 * i + fb
                        if bmi >= 2:
                            s_fev.wait(t, bmi - 1)
                        s_bm.inc(t.matmul(ps_bm[fb % 2][:], wnT[0:64, :],
                                          eb_sb[0:64, 512 * fb:512 * (fb + 1)],
                                          start=True, stop=True), 1)

        assert s_out.cnt == (128 + 5 * 16 if debug_taps else 128) and s_fin.cnt == 8
        assert s_bm.cnt == 64 and s_fev.cnt == 64

    return nc


def _perm():
    p = np.empty(CD, dtype=np.int64)
    for j in range(32):
        R, hl = j // 8, j % 8
        kvl, g = hl // 2, hl % 2
        gh = 16 * g + 4 * R + kvl
        p[128 * j:128 * (j + 1)] = np.arange(128 * gh, 128 * (gh + 1))
    return p


_NC_CACHE = {}


def prepare_in_maps(L0_attn_x, L0_attn_Wq, L0_attn_bq, L0_attn_Wk, L0_attn_bk,
                    L0_attn_Wv, L0_attn_bv, L0_ffn_router_weight,
                    L0_ffn_router_bias, L0_ffn_expert_bias):
    x = np.asarray(L0_attn_x, dtype=np.float32)
    Wq = np.asarray(L0_attn_Wq, dtype=np.float32)
    bq = np.asarray(L0_attn_bq, dtype=np.float32)
    Wk = np.asarray(L0_attn_Wk, dtype=np.float32)
    bk = np.asarray(L0_attn_bk, dtype=np.float32)
    Wv = np.asarray(L0_attn_Wv, dtype=np.float32)
    bv = np.asarray(L0_attn_bv, dtype=np.float32)
    Wr = np.asarray(L0_ffn_router_weight, dtype=np.float32)
    br = np.asarray(L0_ffn_router_bias, dtype=np.float32)
    Eb = np.asarray(L0_ffn_expert_bias, dtype=np.float32)

    perm = _perm()

    Wq_p = Wq[:, perm]
    Wk_p = Wk[:, perm]
    Wv_p = Wv[:, perm]
    Wr_p = Wr[:, perm]
    Eb_p = np.ascontiguousarray(Eb[:, perm])

    wrt = np.ascontiguousarray(
        Wr_p.T.reshape(NCC, 128, E).transpose(1, 0, 2).reshape(128, NCC * E))
    br_bc = np.ascontiguousarray(np.broadcast_to(br[None, :], (128, E)))

    in_maps = []
    for c in range(8):
        b, r = c // 4, c % 4
        x_sl = np.ascontiguousarray(x[b, NTOK * r:NTOK * (r + 1), :][:, perm])
        wt = np.empty((NDS, 128, NCC * 128), dtype=np.float32)
        bias_t = np.empty((128, NDS), dtype=np.float32)
        for ds in range(NDS):
            if ds < 8:
                g, kvl = ds // 4, ds % 4
                h = 16 * g + 4 * r + kvl
                Wsub = Wq_p[128 * h:128 * (h + 1), :]
                bias_t[:, ds] = bq[128 * h:128 * (h + 1)]
            elif ds < 12:
                kvh = 4 * r + (ds - 8)
                Wsub = Wk_p[128 * kvh:128 * (kvh + 1), :]
                bias_t[:, ds] = bk[128 * kvh:128 * (kvh + 1)]
            else:
                kvh = 4 * r + (ds - 12)
                Wsub = Wv_p[128 * kvh:128 * (kvh + 1), :]
                bias_t[:, ds] = bv[128 * kvh:128 * (kvh + 1)]
            wt[ds] = Wsub.T.reshape(NCC, 128, 128).transpose(1, 0, 2).reshape(128, NCC * 128)
        sel = np.zeros((128, 4), dtype=np.float32)
        sel[:, r] = 1.0
        in_maps.append({
            "x_sh": x_sl,
            "wt_in": np.ascontiguousarray(wt),
            "bias_in": np.ascontiguousarray(bias_t),
            "wrt_in": wrt,
            "br_in": br_bc,
            "eb_in": Eb_p,
            "sel_in": sel,
        })

    return in_maps


def kernel(**inputs):
    perm = _perm()
    in_maps = prepare_in_maps(**inputs)
    if "nc" not in _NC_CACHE:
        _NC_CACHE["nc"] = build_program()
    nc = _NC_CACHE["nc"]

    res = run_bass_kernel_spmd(nc, in_maps, core_ids=list(range(8)))

    out = np.empty((B, T, CD), dtype=np.float32)
    for c in range(8):
        b, r = c // 4, c % 4
        tmp = np.empty((NTOK, CD), dtype=np.float32)
        tmp[:, perm] = np.asarray(res.results[c]["y_sh"])
        out[b, NTOK * r:NTOK * (r + 1), :] = tmp
    return out


# version 14
# speedup vs baseline: 1.1049x; 1.1049x over previous
import math
from contextlib import ExitStack

import numpy as np

import concourse.bass as bass
import concourse.mybir as mybir
from concourse.bass_utils import run_bass_kernel_spmd

F32 = mybir.dt.float32
BF16 = mybir.dt.bfloat16
AF = mybir.ActivationFunctionType
ALU = mybir.AluOpType
AX = mybir.AxisListType

B = 2
T = 4096
CD = 4096
NH = 32
NKV = 16
HD = 128
SCALE = 1.0 / math.sqrt(32.0)
E = 64
KTOP = 8
EPS = 1e-5
NTOK = 1024
NDS = 16
NCC = 32
NQC = 8
NEG_BIG = 1.0e30

N_TILES_B = 1152
N_PAIRS_B = 576
N_CHUNKS_B = 64


def _tile_base(hb, qc):
    return 144 * hb + 2 * qc * (qc + 1)


def _pair_base(hb, qc):
    return 72 * hb + qc * (qc + 1)


class Sem:

    def __init__(self, ctx, nc, name):
        self.h = ctx.enter_context(nc.semaphore(name))
        self.cnt = 0

    def inc(self, inst, k=1):
        inst.then_inc(self.h, k)
        self.cnt += k
        return inst

    def wait(self, eng, v):
        if v > 0:
            eng.wait_ge(self.h, v)


def build_program(debug_taps=False):
    nc = bass.Bass("TRN2", target_bir_lowering=False, num_devices=8,
                   detect_race_conditions=False)

    x_sh = nc.dram_tensor("x_sh", [NTOK, CD], F32, kind="ExternalInput").ap()
    wt_in = nc.dram_tensor("wt_in", [NDS, 128, NCC * 128], F32, kind="ExternalInput").ap()
    bias_in = nc.dram_tensor("bias_in", [128, NDS], F32, kind="ExternalInput").ap()
    wrt_in = nc.dram_tensor("wrt_in", [128, NCC * 64], F32, kind="ExternalInput").ap()
    br_in = nc.dram_tensor("br_in", [128, E], F32, kind="ExternalInput").ap()
    eb_in = nc.dram_tensor("eb_in", [E, CD], F32, kind="ExternalInput").ap()
    sel_in = nc.dram_tensor("sel_in", [128, 4], F32, kind="ExternalInput").ap()
    y_sh = nc.dram_tensor("y_sh", [NTOK, CD], F32, kind="ExternalOutput").ap()

    if debug_taps:
        dbg_qkv = nc.dram_tensor("dbg_qkv", [3, 128, T], BF16, kind="ExternalOutput").ap()
        dbg_ag2in = nc.dram_tensor("dbg_ag2in", [8 * T, HD], BF16, kind="ExternalOutput").ap()
        dbg_ag2out = nc.dram_tensor("dbg_ag2out", [T, HD], BF16, kind="ExternalOutput").ap()
        dbg_attnhb = nc.dram_tensor("dbg_attnhb", [128, NQC * 512], BF16, kind="ExternalOutput").ap()
        dbg_rden = nc.dram_tensor("dbg_rden", [128, 32], F32, kind="ExternalOutput").ap()
        dbg_attnf = nc.dram_tensor("dbg_attnf", [128, CD], F32, kind="ExternalOutput").ap()
        dbg_cand = nc.dram_tensor("dbg_cand", [128, CD], BF16, kind="ExternalOutput").ap()
        dbg_lg = nc.dram_tensor("dbg_lg", [128, E], F32, kind="ExternalOutput").ap()
        dbg_wn = nc.dram_tensor("dbg_wn", [128, E], F32, kind="ExternalOutput").ap()
        dbg_hT = nc.dram_tensor("dbg_hT", [128, CD], BF16, kind="ExternalOutput").ap()
        dbg_ex = nc.dram_tensor("dbg_ex", [128, E], F32, kind="ExternalOutput").ap()
        dbg_sacc = nc.dram_tensor("dbg_sacc", [128, 1], F32, kind="ExternalOutput").ap()
        dbg_t8 = nc.dram_tensor("dbg_t8", [128, 1], F32, kind="ExternalOutput").ap()
        dbg_m8 = nc.dram_tensor("dbg_m8", [128, E], F32, kind="ExternalOutput").ap()
        dbg_rS = nc.dram_tensor("dbg_rS", [128, 1], F32, kind="ExternalOutput").ap()
        dbg_wv = nc.dram_tensor("dbg_wv", [128, E], F32, kind="ExternalOutput").ap()
        dbg_tmp = nc.dram_tensor("dbg_tmp", [128, CD], F32, kind="ExternalOutput").ap()
        dbg_x1 = nc.dram_tensor("dbg_x1", [128, CD], F32, kind="ExternalOutput").ap()
        dbg_mv = nc.dram_tensor("dbg_mv", [128, 2], F32, kind="ExternalOutput").ap()
        dbg_rstd = nc.dram_tensor("dbg_rstd", [128, 1], F32, kind="ExternalOutput").ap()

    ag1_in = nc.dram_tensor("ag1_in", [CD, NTOK], BF16, kind="Internal").ap()
    ag1_out = nc.dram_tensor("ag1_out", [4 * CD, NTOK], BF16, kind="Internal").ap()
    qkv = nc.dram_tensor("qkv", [NDS, 128, T], BF16, kind="Internal").ap()
    ag2_in = nc.dram_tensor("ag2_in", [8 * T, HD], BF16, kind="Internal").ap()
    bar_in = nc.dram_tensor("bar_in", [1, 8], F32, kind="Internal").ap()
    bar_out = nc.dram_tensor("bar_out", [1, 8], F32, kind="Internal").ap()
    ag2_out = nc.dram_tensor("ag2_out", [4 * 8 * T, HD], BF16, kind="Internal").ap()

    ctx = ExitStack()
    with ctx:
        def sbuf(name, shape, dtype=F32):
            return ctx.enter_context(nc.sbuf_tensor(name, shape, dtype))

        idn = sbuf("idn", [128, 128], BF16)
        ones1 = sbuf("ones1", [128, 1], BF16)
        onesr = sbuf("onesr", [1, 128], BF16)
        ones8 = sbuf("ones8", [1, 8], F32)
        maskp = sbuf("maskp", [128, 2 * 1024], BF16)
        bias_sb = sbuf("bias_sb", [128, NDS], F32)
        br_b = sbuf("br_b", [128, E], F32)
        sel_sb = sbuf("sel_sb", [128, 4], F32)
        wr_sb = sbuf("wr_sb", [128, NCC * 64], BF16)
        eb_sb = sbuf("eb_sb", [E, CD], BF16)
        epsc = sbuf("epsc", [128, 2], F32)
        negM = sbuf("negM", [128, E], F32)
        w_ctx = ExitStack()
        w_sb = w_ctx.enter_context(nc.sbuf_tensor("w_sb", [128, 8 * CD], BF16))

        sm = lambda n: Sem(ctx, nc, n)
        s_ini = sm("s_ini")
        s_icast = sm("s_icast")
        s_w = sm("s_w")
        s_xin = sm("s_xin")
        s_mv0 = sm("s_mv0")
        s_rs0 = sm("s_rs0")
        s_xn = sm("s_xn")
        s_tr0 = sm("s_tr0")
        s_ev0 = sm("s_ev0")
        s_ag1 = sm("s_ag1")
        s_cc = sm("s_cc")
        s_xnt = sm("s_xnt")
        s_xblk = sm("s_xblk")
        s_pj = sm("s_pj")
        s_evp = sm("s_evp")
        s_trv = sm("s_trv")
        s_evv = sm("s_evv")
        s_qkv = sm("s_qkv")
        s_a2k = sm("s_a2k")
        s_a2sq = sm("s_a2sq")
        s_a2m = sm("s_a2m")
        s_a2r = sm("s_a2r")
        s_a2t = sm("s_a2t")
        s_negm = sm("s_negm")
        s_kv = sm("s_kv")
        s_q = sm("s_q")
        s_hfree = sm("s_hfree")
        s_st = sm("s_st")
        s_exp = sm("s_exp")
        s_msk = sm("s_msk")
        s_pv = sm("s_pv")
        s_oev = sm("s_oev")
        s_dcp = sm("s_dcp")
        s_dh = sm("s_dh")
        s_dtr = sm("s_dtr")
        s_rd = sm("s_rd")
        s_atr = sm("s_atr")
        s_est = sm("s_est")
        s_ag2 = sm("s_ag2")
        s_xc = sm("s_xc")
        s_cand = sm("s_cand")
        s_cand2 = sm("s_cand2")
        s_dbgf = sm("s_dbgf")
        s_dbg2 = sm("s_dbg2")
        s_sel = sm("s_sel")
        s_mv1 = sm("s_mv1"); s_rs1 = sm("s_rs1")
        s_x1c = sm("s_x1c")
        s_mv2 = sm("s_mv2"); s_rs2 = sm("s_rs2")
        s_hbf = sm("s_hbf")
        s_htr = sm("s_htr"); s_hev = sm("s_hev")
        s_lgm = sm("s_lgm"); s_lgr = sm("s_lgr")
        s_ex = sm("s_ex"); s_wn = sm("s_wn")
        s_wtr = sm("s_wtr"); s_wte = sm("s_wte")
        s_bm = sm("s_bm"); s_fev = sm("s_fev")
        s_mv3 = sm("s_mv3"); s_rs3 = sm("s_rs3")
        s_fin = sm("s_fin")
        s_out = sm("s_out")

        with ExitStack() as p0:
            sb0 = lambda n, s, d=F32: p0.enter_context(nc.sbuf_tensor(n, s, d))
            ps0 = lambda n, s, d=F32: p0.enter_context(nc.psum_tensor(n, s, d))
            sb_x = [sb0("sb_x0", [128, CD]), sb0("sb_x1", [128, CD])]
            stats0 = sb0("stats0", [128, 48])
            mv0 = [sb0("mv0a", [128, 2]), sb0("mv0b", [128, 2])]
            lntmp0 = sb0("lntmp0", [128, 1])
            rstd0 = [sb0("rstd0a", [128, 1]), sb0("rstd0b", [128, 1])]
            xn_bf = [sb0("xn0", [128, CD], BF16), sb0("xn1", [128, CD], BF16)]
            xnT_loc = sb0("xnT_loc", [128, NCC * NTOK], BF16)
            ps_tr0 = [ps0("ps_tr0a", [128, 1024], BF16), ps0("ps_tr0b", [128, 1024], BF16)]
            blk = p0.enter_context(nc.Block("p0", no_gpsimd_drain=True))

            @blk.sync
            def _(sync):
                s_ini.inc(sync.dma_start(bias_sb[:], bias_in[:]), 16)
                s_ini.inc(sync.dma_start(br_b[:], br_in[:]), 16)
                s_ini.inc(sync.dma_start(sel_sb[:], sel_in[:]), 16)
                for i in range(8):
                    if i >= 2:
                        s_xn.wait(sync, i - 1)
                    s_xin.inc(sync.dma_start(sb_x[i % 2][:], x_sh[128 * i:128 * (i + 1), :]), 16)
                s_ev0.wait(sync, 32)
                for cc in range(NCC):
                    s_ag1.inc(sync.dma_start(
                        ag1_in[128 * cc:128 * (cc + 1), :],
                        xnT_loc[:, 1024 * cc:1024 * (cc + 1)]), 16)

            @blk.gpsimd
            def _(g):
                g.memset(idn[:], 1.0)
                g.affine_select(out=idn[:], in_=idn[:], compare_op=ALU.is_equal,
                                fill=0.0, base=0, pattern=[[-1, 128]], channel_multiplier=1)
                g.memset(ones1[:], 1.0)
                g.memset(onesr[:], 1.0)
                g.memset(ones8[:], 1.0)
                g.memset(epsc[:, 0:1], EPS)
                g.memset(epsc[:, 1:2], 1e-30)
                g.memset(maskp[:], 1.0)
                for u in range(4):
                    g.affine_select(
                        out=maskp[:, 1024 * (u // 2) + 512 * (u % 2):1024 * (u // 2) + 512 * (u % 2) + 512],
                        in_=maskp[:, 1024 * (u // 2) + 512 * (u % 2):1024 * (u // 2) + 512 * (u % 2) + 512],
                        compare_op=ALU.is_ge, fill=0.0, base=-128 * u,
                        pattern=[[1, 512]], channel_multiplier=-1)
                s_ini.inc(g.engine_nop(), 1)
                s_icast.inc(g.dma_start(wr_sb[:], wrt_in[:]), 16)
                s_icast.inc(g.dma_start(eb_sb[:], eb_in[:]), 16)
                for ds in range(8):
                    s_w.inc(g.dma_start(w_sb[:, CD * ds:CD * (ds + 1)],
                                        wt_in[ds]), 16)
                s_ag1.wait(g, 512)
                s_cc.inc(g.collective_compute(
                    "AllGather", ALU.bypass,
                    replica_groups=[[0, 1, 2, 3], [4, 5, 6, 7]],
                    ins=[ag1_in[:].opt()],
                    outs=[ag1_out[:].opt()]), 1)

            @blk.vector
            def _(v):
                s_ini.wait(v, 49)
                for i in range(8):
                    s_xin.wait(v, 16 * (i + 1))
                    for j in range(8):
                        v.bn_stats(stats0[:, 6 * j:6 * (j + 1)],
                                   sb_x[i % 2][:, 512 * j:512 * (j + 1)])
                    v.bn_stats(stats0[:, 42:48], sb_x[i % 2][:, 3584:4096])
                    v.bn_aggr(mv0[i % 2][:, 0:2], stats0[:, 0:48])
                    s_mv0.inc(v.bn_aggr(mv0[i % 2][:, 0:2], stats0[:, 0:48]), 1)
                    s_rs0.wait(v, i + 1)
                    v.tensor_scalar(
                        xn_bf[i % 2][:], sb_x[i % 2][:],
                        mv0[i % 2][:, 0:1], rstd0[i % 2][:, 0:1],
                        ALU.subtract, ALU.mult)
                    s_xn.inc(v.tensor_scalar(
                        xn_bf[i % 2][:, CD - 256:CD], sb_x[i % 2][:, CD - 256:CD],
                        mv0[i % 2][:, 0:1], rstd0[i % 2][:, 0:1],
                        ALU.subtract, ALU.mult), 1)

            @blk.scalar
            def _(s):
                for i in range(8):
                    s_mv0.wait(s, i + 1)
                    s.activation(lntmp0[:], mv0[i % 2][:, 1:2], AF.Ln, bias=epsc[:, 0:1])
                    s.activation(rstd0[i % 2][:], lntmp0[:], AF.Exp, scale=-0.5)
                    s_rs0.inc(s.activation(rstd0[i % 2][:], lntmp0[:], AF.Exp, scale=-0.5), 1)
                    for grp in range(4):
                        gi = 4 * i + grp
                        s_tr0.wait(s, gi + 1)
                        src = ps_tr0[grp % 2].rearrange("p (j d) -> p j d", d=128)
                        dst = xnT_loc.rearrange("p (c t) -> p c t", t=NTOK)[
                            :, 8 * grp:8 * grp + 8, 128 * i:128 * i + 128]
                        s_ev0.inc(s.activation(dst, src, AF.Copy), 1)

            @blk.tensor
            def _(t):
                for i in range(8):
                    s_xn.wait(t, i + 1)
                    for grp in range(4):
                        gi = 4 * i + grp
                        if gi >= 2:
                            s_ev0.wait(t, gi - 1)
                        for j in list(range(8)) + [7]:
                            cc = 8 * grp + j
                            last = t.transpose(
                                ps_tr0[grp % 2][:, 128 * j:128 * (j + 1)],
                                xn_bf[i % 2][:, 128 * cc:128 * (cc + 1)], idn[:])
                        s_tr0.inc(last, 1)

        assert s_xin.cnt == 128 and s_ag1.cnt == 512 and s_ev0.cnt == 32

        with ExitStack() as p1:
            sb1 = lambda n, s, d=F32: p1.enter_context(nc.sbuf_tensor(n, s, d))
            ps1 = lambda n, s, d=F32: p1.enter_context(nc.psum_tensor(n, s, d))
            xnT_blk = [sb1("xnTb0", [128, NCC * 512], BF16), sb1("xnTb1", [128, NCC * 512], BF16)]
            stg = [sb1(f"stg{k}", [128, 512], BF16) for k in range(4)]
            stgV = [sb1(f"stgV{k}", [128, 512], BF16) for k in range(2)]
            stg2 = [sb1(f"stg2{k}", [128, 512], BF16) for k in range(2)]
            ps_proj = [ps1("ps_pj0", [128, 512]), ps1("ps_pj1", [128, 512])]
            ps_trv = [ps1("ps_trv0", [128, 512], BF16), ps1("ps_trv1", [128, 512], BF16)]
            blk = p1.enter_context(nc.Block("p1", no_gpsimd_drain=True))

            def v_of(blkid, ds):
                return (blkid - 8) * 4 + (ds - 4)

            vidx_to_idx = {}
            for blkid in range(8, 16):
                for ds in range(4, 8):
                    vidx_to_idx[v_of(blkid, ds)] = blkid * 8 + ds

            @blk.sync
            def _(sync):
                s_cc.wait(sync, 1)

                def load_block(blkid):
                    tb = blkid % 8
                    for cc in range(NCC):
                        base = 4096 * (tb // 2) + 128 * cc
                        s_xnt.inc(sync.dma_start(
                            xnT_blk[blkid % 2][:, 512 * cc:512 * (cc + 1)],
                            ag1_out[base:base + 128,
                                    512 * (tb % 2):512 * (tb % 2) + 512]), 16)

                load_block(0)
                load_block(1)
                for blkid in range(16):
                    for ds in range(8):
                        gds = (blkid // 8) * 8 + ds
                        idx = blkid * 8 + ds
                        if gds < 12:
                            s_evp.wait(sync, idx + 1)
                            src = stg[idx % 4]
                        else:
                            vi = v_of(blkid, ds)
                            s_evv.wait(sync, vi + 1)
                            src = stg2[vi % 2]
                        tb = blkid % 8
                        s_qkv.inc(sync.dma_start(
                            qkv[gds][:, 512 * tb:512 * (tb + 1)], src[:]), 16)
                    if blkid + 2 <= 15:
                        s_evp.wait(sync, 8 * (blkid + 1))
                        load_block(blkid + 2)

            @blk.gpsimd
            def _(g):
                s_evp.wait(g, 64)
                for ds in range(8):
                    s_w.inc(g.dma_start(w_sb[:, CD * ds:CD * (ds + 1)],
                                        wt_in[8 + ds]), 16)

            @blk.tensor
            def _(t):
                for blkid in range(16):
                    s_xnt.wait(t, 512 * (blkid + 1))
                    for ds in range(8):
                        gds = (blkid // 8) * 8 + ds
                        idx = blkid * 8 + ds
                        s_w.wait(t, 16 * ((blkid // 8) * 8 + ds + 1))
                        if idx >= 2:
                            s_evp.wait(t, idx - 1)
                        for cc in range(NCC):
                            last = t.matmul(
                                ps_proj[idx % 2][:],
                                w_sb[:, CD * ds + 128 * cc:CD * ds + 128 * (cc + 1)],
                                xnT_blk[blkid % 2][:, 512 * cc:512 * (cc + 1)],
                                start=(cc == 0), stop=(cc == NCC - 1))
                        s_pj.inc(last, 1)
                    if blkid >= 8:
                        for ds in range(4, 8):
                            idx = blkid * 8 + ds
                            vi = v_of(blkid, ds)
                            s_evp.wait(t, idx + 1)
                            s_evv.wait(t, vi - 1)
                            for sb_i in [0, 1, 2, 3, 3]:
                                last = t.transpose(
                                    ps_trv[vi % 2][:, 128 * sb_i:128 * (sb_i + 1)],
                                    stgV[vi % 2][:, 128 * sb_i:128 * (sb_i + 1)],
                                    idn[:])
                            s_trv.inc(last, 1)

            @blk.scalar
            def _(s):
                for blkid in range(16):
                    for ds in range(8):
                        gds = (blkid // 8) * 8 + ds
                        idx = blkid * 8 + ds
                        s_pj.wait(s, idx + 1)
                        if gds < 12:
                            if idx >= 4:
                                s_qkv.wait(s, 16 * (idx - 3))
                            s_evp.inc(s.add(stg[idx % 4][:], ps_proj[idx % 2][:],
                                            bias_sb[:, gds:gds + 1]), 1)
                        else:
                            vi = v_of(blkid, ds)
                            if vi >= 2:
                                s_trv.wait(s, vi - 1)
                            s_evp.inc(s.add(stgV[vi % 2][:], ps_proj[idx % 2][:],
                                            bias_sb[:, gds:gds + 1]), 1)
                    if blkid >= 8:
                        for ds in range(4, 8):
                            vi = v_of(blkid, ds)
                            s_trv.wait(s, vi + 1)
                            if vi >= 2:
                                s_qkv.wait(s, 16 * (vidx_to_idx[vi - 2] + 1))
                            s_evv.inc(s.activation(
                                stg2[vi % 2][:], ps_trv[vi % 2][:], AF.Copy), 1)

        assert s_qkv.cnt == 16 * 128 and s_pj.cnt == 128
        assert s_evv.cnt == 32 and s_trv.cnt == 32
        w_ctx.close()

        with ExitStack() as p2:
            sb2 = lambda n, s, d=F32: p2.enter_context(nc.sbuf_tensor(n, s, d))
            ps2_ = lambda n, s, d=F32: p2.enter_context(nc.psum_tensor(n, s, d))
            a2buf = [sb2("a2b0", [128, T], BF16), sb2("a2b1", [128, T], BF16)]
            sqbuf = [sb2("a2s0", [128, T], BF16), sb2("a2s1", [128, T], BF16)]
            kmax2 = sb2("kmax2", [1, 4])
            km8 = sb2("km8", [1, 8])
            qm2 = sb2("qm2", [1, E])
            k2row = sb2("k2row", [1, E])
            m2row = sb2("m2row", [1, E])
            mrow = sb2("mrow", [1, E])
            mrow2 = sb2("mrow2", [1, E])
            negrow = sb2("negrow", [1, E])
            negbf = sb2("negbf", [1, E], BF16)
            tmp11 = sb2("tmp11", [1, 1])
            ps_nrm = ps2_("ps_nrm", [1, 512])
            ps_bc = ps2_("ps_bc", [128, E])
            blk = p2.enter_context(nc.Block("p2", no_gpsimd_drain=True))

            @blk.sync
            def _(sync):
                for j in range(12):
                    if j >= 2:
                        s_a2sq.wait(sync, j - 1)
                    gds = 8 + j if j < 4 else j - 4
                    s_a2k.inc(sync.dma_start(a2buf[j % 2][:], qkv[gds][:, :]), 16)

            @blk.vector
            def _(v):
                for j in range(12):
                    s_a2k.wait(v, 16 * (j + 1))
                    if j >= 2:
                        s_a2m.wait(v, 8 * (j - 1))
                    v.tensor_mul(sqbuf[j % 2][:], a2buf[j % 2][:], a2buf[j % 2][:])
                    s_a2sq.inc(v.tensor_mul(sqbuf[j % 2][:, T - 512:T],
                                            a2buf[j % 2][:, T - 512:T],
                                            a2buf[j % 2][:, T - 512:T]), 1)
                    for c8 in range(8):
                        s_a2m.wait(v, 8 * j + c8 + 1)
                        if j < 4:
                            inst = v.reduce_max(km8[0:1, c8:c8 + 1], ps_nrm[0:1, :], AX.X)
                        else:
                            inst = v.reduce_max(qm2[0:1, 8 * (j - 4) + c8:8 * (j - 4) + c8 + 1],
                                                ps_nrm[0:1, :], AX.X)
                        s_a2r.inc(inst, 1)
                    if j < 4:
                        v.reduce_max(kmax2[0:1, j:j + 1], km8[0:1, 0:8], AX.X)
                        v.reduce_max(kmax2[0:1, j:j + 1], km8[0:1, 0:8], AX.X)
                s_a2r.wait(v, 96)
                for h in range(8):
                    v.tensor_scalar(k2row[0:1, 8 * h:8 * (h + 1)], ones8[0:1, :],
                                    kmax2[0:1, h % 4:h % 4 + 1], None, ALU.mult)
                v.tensor_mul(m2row[0:1, :], qm2[0:1, :], k2row[0:1, :])
                s_a2t.inc(v.tensor_mul(m2row[0:1, :], qm2[0:1, :], k2row[0:1, :]), 1)

            @blk.scalar
            def _(s):
                s_a2t.wait(s, 1)
                s.activation(mrow2[0:1, :], m2row[0:1, :], AF.Ln, bias=epsc[0:1, 1:2])
                s_a2t.inc(s.activation(mrow[0:1, :], mrow2[0:1, :], AF.Exp, scale=0.5), 1)

            @blk.vector
            def _(v):
                s_a2t.wait(v, 2)
                v.tensor_scalar(negrow[0:1, :], mrow[0:1, :], -SCALE, None, ALU.mult)
                v.tensor_copy(negbf[0:1, :], negrow[0:1, :])
                s_a2t.inc(v.tensor_copy(negbf[0:1, :], negrow[0:1, :]), 1)

            @blk.tensor
            def _(t):
                for j in range(12):
                    s_a2sq.wait(t, j + 1)
                    for c8 in range(8):
                        if 8 * j + c8 >= 1:
                            s_a2r.wait(t, 8 * j + c8)
                        s_a2m.inc(t.matmul(ps_nrm[0:1, :], ones1[:],
                                           sqbuf[j % 2][:, 512 * c8:512 * (c8 + 1)],
                                           start=True, stop=True), 1)
                s_a2t.wait(t, 3)
                s_a2m.inc(t.matmul(ps_bc[:], onesr[0:1, :], negbf[0:1, :],
                                   start=True, stop=True), 1)

            @blk.vector
            def _(v):
                s_a2m.wait(v, 97)
                v.tensor_copy(negM[:], ps_bc[:])
                s_negm.inc(v.tensor_copy(negM[:], ps_bc[:]), 1)

        with ExitStack() as p3:
            sb3 = lambda n, s, d=F32: p3.enter_context(nc.sbuf_tensor(n, s, d))
            ps3 = lambda n, s, d=F32: p3.enter_context(nc.psum_tensor(n, s, d))
            q_sb = [sb3("q_sb0", [128, T], BF16), sb3("q_sb1", [128, T], BF16)]
            k_sb = [sb3("k_sb0", [128, T], BF16), sb3("k_sb1", [128, T], BF16)]
            v_sb = [sb3("v_sb0", [128, T], BF16), sb3("v_sb1", [128, T], BF16)]
            pt = [sb3("pt0", [128, 1024], BF16), sb3("pt1", [128, 1024], BF16)]
            ptm = [sb3("ptm0", [128, 1024], BF16), sb3("ptm1", [128, 1024], BF16)]
            attn_hb = [sb3("attn0", [128, NQC * 512], BF16), sb3("attn1", [128, NQC * 512], BF16)]
            den_stage = [sb3("denst0", [1, 512]), sb3("denst1", [1, 512])]
            den_hb = [sb3("denh0", [128, 512], BF16), sb3("denh1", [128, 512], BF16)]
            rden = [sb3("rden0", [128, 32]), sb3("rden1", [128, 32])]
            rden_raw = sb3("rden_raw", [128, 32])
            estg = [sb3("estg0", [128, 512], BF16), sb3("estg1", [128, 512], BF16)]
            ps_st = [ps3("ps_st0", [128, 1024]), ps3("ps_st1", [128, 1024])]
            ps_out = [ps3("ps_out0", [128, 512]), ps3("ps_out1", [128, 512])]
            ps_den = ps3("ps_den", [1, 512])
            ps_ep = ps3("ps_ep", [128, 512], BF16)
            blk = p3.enter_context(nc.Block("p3", no_gpsimd_drain=True))

            def gds_of(hb):
                return (hb % 2) * 4 + hb // 2

            @blk.sync
            def _(sync):
                def loads(h):
                    kvl, g = h // 2, h % 2
                    if g == 0:
                        if kvl >= 2:
                            s_oev.wait(sync, 16 * kvl - 16)
                        s_kv.inc(sync.dma_start(k_sb[kvl % 2][:], qkv[8 + kvl][:, :]), 16)
                        s_kv.inc(sync.dma_start(v_sb[kvl % 2][:], qkv[12 + kvl][:, :]), 16)
                    if h >= 2:
                        s_oev.wait(sync, 8 * (h - 1))
                    s_q.inc(sync.dma_start(q_sb[h % 2][:], qkv[gds_of(h)][:, :]), 16)

                loads(0)
                loads(1)
                for hb in range(8):
                    if hb + 2 <= 7:
                        loads(hb + 2)
                    for qc in range(NQC):
                        gc = 8 * hb + qc
                        s_est.wait(sync, gc + 1)
                        dst = ag2_in.rearrange("(a p) d -> p a d", p=128)[
                            :, 32 * hb + 4 * qc: 32 * hb + 4 * qc + 4, :]
                        s_ag2.inc(sync.dma_start(dst, estg[gc % 2].rearrange(
                            "p (s d) -> p s d", d=128)), 16)
                if debug_taps:
                    s_est.wait(sync, 64)
                    s_ag2.inc(sync.dma_start(dbg_attnhb[:], attn_hb[1][:]), 16)
                    s_ag2.inc(sync.dma_start(dbg_rden[:], rden[1][:]), 16)

            @blk.gpsimd
            def _(g):
                for gc in range(N_CHUNKS_B):
                    s_dcp.wait(g, gc + 1)
                    hb, qc = gc // 8, gc % 8
                    s_dh.inc(g.dma_start(den_hb[hb % 2][qc:qc + 1, :],
                                         den_stage[gc % 2][0:1, :]), 16)
                    if gc == 39:
                        s_ag2.wait(g, 16 * 32)
                        s_cc.inc(g.collective_compute(
                            "AllGather", ALU.bypass,
                            replica_groups=[[0, 1, 2, 3], [4, 5, 6, 7]],
                            ins=[ag2_in[0:16384, :].opt()],
                            outs=[ag2_out[0:65536, :].opt()]), 1)
                s_ag2.wait(g, 16 * 64)
                s_cc.inc(g.collective_compute(
                    "AllGather", ALU.bypass,
                    replica_groups=[[0, 1, 2, 3], [4, 5, 6, 7]],
                    ins=[ag2_in[16384:32768, :].opt()],
                    outs=[ag2_out[65536:131072, :].opt()]), 1)

            @blk.tensor
            def _(t):
                s_negm.wait(t, 1)

                def pvden(hb, qc, q):
                    kvl = hb // 2
                    gp = _pair_base(hb, qc) + q
                    tb_ = _tile_base(hb, qc)
                    nt = 4 * qc + 4
                    np_ = 2 * qc + 2
                    gc = 8 * hb + qc
                    if q == 0:
                        if gc >= 2:
                            s_oev.wait(t, gc - 1)
                        if gc >= 1:
                            s_dcp.wait(t, gc)
                    s_exp.wait(t, gp + 1)
                    diag = q >= np_ - 2
                    if diag:
                        d_no = 2 * gc + (q - (np_ - 2)) + 1
                        s_msk.wait(t, d_no)
                    pbuf = ptm[gp % 2] if diag else pt[gp % 2]
                    for half in range(2):
                        tt = 2 * q + half
                        t.matmul(ps_out[qc % 2][:],
                                 v_sb[kvl % 2][:, 128 * tt:128 * (tt + 1)],
                                 pbuf[:, 512 * half:512 * (half + 1)],
                                 start=(tt == 0), stop=(tt == nt - 1))
                        last = t.matmul(ps_den[0:1, :], ones1[:],
                                        pbuf[:, 512 * half:512 * (half + 1)],
                                        start=(tt == 0), stop=(tt == nt - 1))
                        s_pv.inc(last, 1)

                def epilogue_piece(hb, qc):
                    gc = 8 * hb + qc
                    if qc == 0:
                        s_dh.wait(t, 16 * (8 * hb + 8))
                        if hb >= 1:
                            s_rd.wait(t, hb)
                            s_est.wait(t, 8 * hb)
                        for tt in [0, 1, 2, 3, 3]:
                            last = t.transpose(ps_ep[:, 128 * tt:128 * (tt + 1)],
                                               den_hb[hb % 2][:, 128 * tt:128 * (tt + 1)],
                                               idn[:])
                        s_dtr.inc(last, 1)
                    s_oev.wait(t, gc + 1)
                    if qc == 0:
                        s_rd.wait(t, hb + 1)
                    else:
                        s_est.wait(t, gc)
                    for sb_i in [0, 1, 2, 3, 3]:
                        last = t.transpose(
                            ps_ep[:, 128 * sb_i:128 * (sb_i + 1)],
                            attn_hb[hb % 2][:, 512 * qc + 128 * sb_i:512 * qc + 128 * (sb_i + 1)],
                            idn[:])
                    s_atr.inc(last, 1)

                for hb in range(8):
                    kvl = hb // 2
                    s_kv.wait(t, 32 * (kvl + 1))
                    s_q.wait(t, 16 * (hb + 1))
                    for qc in range(NQC):
                        if hb >= 1:
                            epilogue_piece(hb - 1, qc)
                        np_ = 2 * qc + 2
                        pbase = _pair_base(hb, qc)
                        tbase = _tile_base(hb, qc)
                        for p in range(np_):
                            gp = pbase + p
                            if gp >= 2:
                                s_exp.wait(t, gp - 1)
                            for half in range(2):
                                tt = 2 * p + half
                                last = t.matmul(
                                    ps_st[gp % 2][:, 512 * half:512 * (half + 1)],
                                    k_sb[kvl % 2][:, 128 * tt:128 * (tt + 1)],
                                    q_sb[hb % 2][:, 512 * qc:512 * (qc + 1)],
                                    start=True, stop=True)
                                s_st.inc(last, 1)
                            if p >= 1:
                                pvden(hb, qc, p - 1)
                        pvden(hb, qc, np_ - 1)
                for qc in range(NQC):
                    epilogue_piece(7, qc)

            @blk.scalar
            def _(s):
                s_negm.wait(s, 1)
                for hb in range(8):
                    gds = gds_of(hb)
                    for qc in range(NQC):
                        gc = 8 * hb + qc
                        np_ = 2 * qc + 2
                        pbase = _pair_base(hb, qc)
                        tbase = _tile_base(hb, qc)
                        for p in range(np_):
                            gp = pbase + p
                            s_st.wait(s, tbase + 2 * p + 2)
                            if gp >= 2:
                                s_pv.wait(s, 2 * gp - 2)
                            s_exp.inc(s.activation(
                                pt[gp % 2][:], ps_st[gp % 2][:, 0:1024], AF.Exp,
                                scale=SCALE,
                                bias=negM[:, 8 * gds + qc:8 * gds + qc + 1]), 1)
                        s_pv.wait(s, tbase + 4 * qc + 4)
                        s_oev.inc(s.activation(
                            attn_hb[hb % 2][:, 512 * qc:512 * (qc + 1)],
                            ps_out[qc % 2][:], AF.Copy), 1)

            @blk.vector
            def _(v):
                for hb in range(8):
                    for qc in range(NQC):
                        gc = 8 * hb + qc
                        np_ = 2 * qc + 2
                        pbase = _pair_base(hb, qc)
                        tbase = _tile_base(hb, qc)
                        if hb >= 1:
                            if qc == 0:
                                s_dtr.wait(v, hb)
                                for tt in range(4):
                                    v.tensor_scalar(rden_raw[:, 8 * tt:8 * (tt + 1)],
                                                    ps_ep[:, 128 * tt:128 * tt + 8],
                                                    1e-37, None, ALU.add)
                                v.reciprocal(rden[(hb - 1) % 2][:, 0:32],
                                             rden_raw[:, 0:32])
                                s_rd.inc(v.reciprocal(rden[(hb - 1) % 2][:, 0:32],
                                                      rden_raw[:, 0:32]), 1)
                            gcp = 8 * (hb - 1) + qc
                            s_atr.wait(v, gcp + 1)
                            if gcp >= 2:
                                s_ag2.wait(v, 16 * (gcp - 1))
                            for sb_i in [0, 1, 2, 3, 3]:
                                last = v.tensor_scalar(
                                    estg[gcp % 2][:, 128 * sb_i:128 * (sb_i + 1)],
                                    ps_ep[:, 128 * sb_i:128 * (sb_i + 1)],
                                    rden[(hb - 1) % 2][:, 8 * sb_i + qc:8 * sb_i + qc + 1],
                                    None, ALU.mult)
                            s_est.inc(last, 1)
                        for dq in range(2):
                            p = np_ - 2 + dq
                            gp = pbase + p
                            s_exp.wait(v, gp + 1)
                            if gp >= 2:
                                s_pv.wait(v, 2 * gp - 2)
                            v.tensor_mul(ptm[gp % 2][:], pt[gp % 2][:],
                                         maskp[:, 1024 * dq:1024 * (dq + 1)])
                            s_msk.inc(v.tensor_mul(
                                ptm[gp % 2][:, 896:1024], pt[gp % 2][:, 896:1024],
                                maskp[:, 1024 * dq + 896:1024 * dq + 1024]), 1)
                        s_pv.wait(v, tbase + 4 * qc + 4)
                        if gc >= 2:
                            s_dh.wait(v, 16 * (gc - 1))
                        v.tensor_copy(den_stage[gc % 2][0:1, :], ps_den[0:1, :])
                        s_dcp.inc(v.tensor_copy(den_stage[gc % 2][0:1, 384:512],
                                                ps_den[0:1, 384:512]), 1)
                for qc in range(NQC):
                    if qc == 0:
                        s_dtr.wait(v, 8)
                        for tt in range(4):
                            v.tensor_scalar(rden_raw[:, 8 * tt:8 * (tt + 1)],
                                            ps_ep[:, 128 * tt:128 * tt + 8],
                                            1e-37, None, ALU.add)
                        v.reciprocal(rden[7 % 2][:, 0:32], rden_raw[:, 0:32])
                        s_rd.inc(v.reciprocal(rden[7 % 2][:, 0:32], rden_raw[:, 0:32]), 1)
                    gcp = 56 + qc
                    s_atr.wait(v, gcp + 1)
                    s_ag2.wait(v, 16 * (gcp - 1))
                    for sb_i in [0, 1, 2, 3, 3]:
                        last = v.tensor_scalar(
                            estg[gcp % 2][:, 128 * sb_i:128 * (sb_i + 1)],
                            ps_ep[:, 128 * sb_i:128 * (sb_i + 1)],
                            rden[7 % 2][:, 8 * sb_i + qc:8 * sb_i + qc + 1],
                            None, ALU.mult)
                    s_est.inc(last, 1)

        assert s_st.cnt == N_TILES_B and s_pv.cnt == N_TILES_B
        assert s_exp.cnt == N_PAIRS_B and s_msk.cnt == 2 * N_CHUNKS_B
        assert s_oev.cnt == N_CHUNKS_B and s_est.cnt == N_CHUNKS_B
        assert s_ag2.cnt == 16 * (N_CHUNKS_B + 2 if debug_taps else N_CHUNKS_B)

        with ExitStack() as p4:
            sb4 = lambda n, s, d=F32: p4.enter_context(nc.sbuf_tensor(n, s, d))
            ps4 = lambda n, s, d=F32: p4.enter_context(nc.psum_tensor(n, s, d))
            x_sb = [sb4("x_sb0", [128, CD])]
            cand = [sb4(f"cand{k}", [128, CD], BF16) for k in range(4)]
            attn_f = [sb4("attnf0", [128, CD]), sb4("attnf1", [128, CD])]
            stats4 = sb4("stats4", [128, 48])
            mvx = [sb4("mvx0", [128, 2]), sb4("mvx1", [128, 2])]
            rstdx = [sb4("rstdx0", [128, 1]), sb4("rstdx1", [128, 1])]
            lntmp4 = sb4("lntmp4", [128, 1])
            tmp_t = sb4("tmp_t", [128, CD])
            x1_sb = [sb4("x1_0", [128, CD])]
            h_f = sb4("h_f", [128, CD])
            h_bf = sb4("h_bf", [128, CD], BF16)
            hT = [sb4("hT0", [128, CD], BF16)]
            lg = sb4("lg", [128, E])
            negmx = sb4("negmx", [128, 1])
            ex = sb4("ex", [128, E])
            sacc = sb4("sacc", [128, 1])
            rwork = sb4("rwork", [128, E])
            rwork2 = sb4("rwork2", [128, E])
            m8 = sb4("m8", [128, 1])
            mask8 = sb4("mask8", [128, E])
            t8 = sb4("t8", [128, 1])
            wv = sb4("wv", [128, E])
            rS = sb4("rS", [128, 1])
            wn = sb4("wn", [128, E])
            wsum = sb4("wsum", [128, 1])
            wn_bf = sb4("wn_bf", [128, E], BF16)
            wnT = sb4("wnT", [64, 128], BF16)
            ffn = sb4("ffn", [128, CD])
            out_sb = [sb4("out0", [128, CD])]
            ps_htr = [ps4("ps_htr0", [128, 1024], BF16), ps4("ps_htr1", [128, 1024], BF16)]
            ps_log = ps4("ps_log", [128, E])
            ps_wtr = ps4("ps_wtr", [64, 128], BF16)
            ps_bm = [ps4("ps_bm0", [128, 512]), ps4("ps_bm1", [128, 512])]
            blk = p4.enter_context(nc.Block("p4", no_gpsimd_drain=True))

            @blk.sync
            def _(sync):
                s_cc.wait(sync, 3)
                s_ini.wait(sync, 49)
                for i in range(8):
                    if i == 1 and debug_taps:
                        s_x1c.wait(sync, 1)
                        s_dbg2.inc(sync.dma_start(dbg_tmp[:], tmp_t[:]), 16)
                        s_dbg2.inc(sync.dma_start(dbg_x1[:], x1_sb[0][:]), 16)
                        s_dbg2.inc(sync.dma_start(dbg_mv[:], mvx[0][:, 0:2]), 16)
                        s_dbg2.inc(sync.dma_start(dbg_rstd[:], rstdx[0][:]), 16)
                        s_dbgf.inc(sync.dma_start(dbg_cand[:], cand[0][:]), 16)
                        s_sel.wait(sync, 1)
                        s_dbgf.inc(sync.dma_start(dbg_attnf[:], attn_f[1][:]), 16)
                        s_lgr.wait(sync, 1)
                        s_dbgf.inc(sync.dma_start(dbg_lg[:], lg[:]), 16)
                        s_wn.wait(sync, 1)
                        s_dbgf.inc(sync.dma_start(dbg_wn[:], wn[:]), 16)
                        s_hbf.wait(sync, 1)
                        s_dbgf.inc(sync.dma_start(dbg_hT[:], hT[0][:]), 16)
                        s_dbgf.inc(sync.dma_start(dbg_ex[:], ex[:]), 16)
                        s_dbgf.inc(sync.dma_start(dbg_sacc[:], sacc[:]), 16)
                        s_dbgf.inc(sync.dma_start(dbg_t8[:], t8[:]), 16)
                        s_dbgf.inc(sync.dma_start(dbg_m8[:], mask8[:]), 16)
                        s_dbgf.inc(sync.dma_start(dbg_rS[:], rS[:]), 16)
                        s_dbgf.inc(sync.dma_start(dbg_wv[:], wv[:]), 16)
                    if i >= 1:
                        s_x1c.wait(sync, i)
                    s_xc.inc(sync.dma_start(x_sb[0][:], x_sh[128 * i:128 * (i + 1), :]), 16)
                    if i >= 1:
                        s_sel.wait(sync, i)
                    for cnd in range(4):
                        for R in range(4):
                            for k in range(2):
                                tok = 1024 * cnd + 128 * i
                                hview = ag2_out.rearrange(
                                    "(k r h t) d -> k r h t d", k=2, r=4, h=4)[
                                    k, R, :, tok:tok + 128, :].transpose([1, 0, 2])
                                dst = cand[cnd].rearrange(
                                    "p (r k h d) -> p r k h d", r=4, k=2, h=4)[
                                    :, R, k, :, :]
                                s_cand.inc(sync.dma_start(dst, hview), 16)
                    if i >= 1:
                        s_fin.wait(sync, i)
                        s_out.inc(sync.dma_start(y_sh[128 * (i - 1):128 * i, :],
                                                 out_sb[0][:]), 16)
                s_fin.wait(sync, 8)
                s_out.inc(sync.dma_start(y_sh[128 * 7:128 * 8, :], out_sb[0][:]), 16)
                if debug_taps:
                    for j, gds in enumerate([0, 8, 12]):
                        s_out.inc(sync.dma_start(dbg_qkv[j], qkv[gds][:, :]), 16)
                    s_out.inc(sync.dma_start(dbg_ag2in[:], ag2_in[:]), 16)
                    s_out.inc(sync.dma_start(dbg_ag2out[:], ag2_out[0:T, :]), 16)
                    sync.wait_ge(s_out.h, 128 + 5 * 16)
                else:
                    sync.wait_ge(s_out.h, 128)

            @blk.vector
            def _(v):
                s_ini.wait(v, 49)
                for i in range(8):
                    s_cand.wait(v, 512 * (i + 1))
                    if i == 1 and debug_taps:
                        s_dbgf.wait(v, 176)
                    v.tensor_scalar(attn_f[0][:], cand[0][:],
                                    sel_sb[:, 0:1], None, ALU.mult)
                    for cnd in range(1, 4):
                        last = v.scalar_tensor_tensor(
                            attn_f[cnd % 2][:], cand[cnd][:], sel_sb[:, cnd:cnd + 1],
                            attn_f[(cnd + 1) % 2][:], ALU.mult, ALU.add)
                    s_sel.inc(last, 1)
                    for j in range(8):
                        v.bn_stats(stats4[:, 6 * j:6 * (j + 1)],
                                   attn_f[1][:, 512 * j:512 * (j + 1)])
                    v.bn_stats(stats4[:, 42:48], attn_f[1][:, 3584:4096])
                    v.bn_aggr(mvx[0][:, 0:2], stats4[:, 0:48])
                    s_mv1.inc(v.bn_aggr(mvx[0][:, 0:2], stats4[:, 0:48]), 1)
                    s_rs1.wait(v, i + 1)
                    v.tensor_scalar(tmp_t[:], attn_f[1][:], mvx[0][:, 0:1],
                                    rstdx[0][:, 0:1], ALU.subtract, ALU.mult)
                    s_xc.wait(v, 16 * (i + 1))
                    s_x1c.inc(v.tensor_add(x1_sb[0][:], tmp_t[:], x_sb[0][:]), 1)
                    if i == 0 and debug_taps:
                        s_dbg2.wait(v, 64)
                    for j in range(8):
                        v.bn_stats(stats4[:, 6 * j:6 * (j + 1)],
                                   x1_sb[0][:, 512 * j:512 * (j + 1)])
                    v.bn_stats(stats4[:, 42:48], x1_sb[0][:, 3584:4096])
                    v.bn_aggr(mvx[1][:, 0:2], stats4[:, 0:48])
                    s_mv2.inc(v.bn_aggr(mvx[1][:, 0:2], stats4[:, 0:48]), 1)
                    s_rs2.wait(v, i + 1)
                    v.tensor_scalar(h_f[:], x1_sb[0][:], mvx[1][:, 0:1],
                                    rstdx[1][:, 0:1], ALU.subtract, ALU.mult)
                    v.tensor_copy(h_bf[:], h_f[:])
                    s_hbf.inc(v.tensor_copy(h_bf[:, CD - 256:CD], h_f[:, CD - 256:CD]), 1)
                    s_lgm.wait(v, i + 1)
                    v.tensor_add(lg[:], ps_log[:], br_b[:])
                    v.reduce_max(negmx[:, 0:1], lg[:], AX.X, negate=True)
                    s_lgr.inc(v.reduce_max(negmx[:, 0:1], lg[:], AX.X, negate=True), 1)
                    s_ex.wait(v, i + 1)
                    v.tensor_copy(rwork[:], ex[:])
                    rw = [rwork, rwork2]
                    for it in range(KTOP - 1):
                        v.reduce_max(m8[:, 0:1], rw[it % 2][:], AX.X)
                        v.reduce_max(m8[:, 0:1], rw[it % 2][:], AX.X)
                        v.tensor_scalar(mask8[:], rw[it % 2][:], m8[:, 0:1], None, ALU.is_ge)
                        v.scalar_tensor_tensor(rw[(it + 1) % 2][:], mask8[:], -NEG_BIG,
                                               rw[it % 2][:], ALU.mult, ALU.add)
                    v.reduce_max(t8[:, 0:1], rw[(KTOP - 1) % 2][:], AX.X)
                    v.reduce_max(t8[:, 0:1], rw[(KTOP - 1) % 2][:], AX.X)
                    v.tensor_scalar(mask8[:], ex[:], t8[:, 0:1], None, ALU.is_ge)
                    v.tensor_mul(wv[:], ex[:], mask8[:])
                    v.reciprocal(rS[:, 0:1], sacc[:, 0:1])
                    v.reciprocal(rS[:, 0:1], sacc[:, 0:1])
                    v.tensor_scalar(wn[:], wv[:], rS[:, 0:1], None, ALU.mult)
                    v.reduce_sum(wsum[:, 0:1], wn[:], AX.X)
                    v.reduce_sum(wsum[:, 0:1], wn[:], AX.X)
                    v.tensor_copy(wn_bf[:], wn[:])
                    s_wn.inc(v.tensor_copy(wn_bf[:], wn[:]), 1)
                    for fb in range(8):
                        s_bm.wait(v, 8 * i + fb + 1)
                        s_fev.inc(v.scalar_tensor_tensor(
                            ffn[:, 512 * fb:512 * (fb + 1)],
                            h_f[:, 512 * fb:512 * (fb + 1)], wsum[:, 0:1],
                            ps_bm[fb % 2][:], ALU.mult, ALU.add), 1)
                    for j in range(8):
                        v.bn_stats(stats4[:, 6 * j:6 * (j + 1)],
                                   ffn[:, 512 * j:512 * (j + 1)])
                    v.bn_stats(stats4[:, 42:48], ffn[:, 3584:4096])
                    v.bn_aggr(mvx[0][:, 0:2], stats4[:, 0:48])
                    s_mv3.inc(v.bn_aggr(mvx[0][:, 0:2], stats4[:, 0:48]), 1)
                    s_rs3.wait(v, i + 1)
                    v.tensor_scalar(tmp_t[:], ffn[:], mvx[0][:, 0:1],
                                    rstdx[0][:, 0:1], ALU.subtract, ALU.mult)
                    s_out.wait(v, 16 * i)
                    v.tensor_add(out_sb[0][:], tmp_t[:], x1_sb[0][:])
                    s_fin.inc(v.tensor_add(out_sb[0][:, CD - 256:CD],
                                           tmp_t[:, CD - 256:CD],
                                           x1_sb[0][:, CD - 256:CD]), 1)

            @blk.scalar
            def _(s):
                for i in range(8):
                    s_mv1.wait(s, i + 1)
                    s.activation(lntmp4[:], mvx[0][:, 1:2], AF.Ln, bias=epsc[:, 0:1])
                    s.activation(rstdx[0][:], lntmp4[:], AF.Exp, scale=-0.5)
                    s_rs1.inc(s.activation(rstdx[0][:], lntmp4[:], AF.Exp, scale=-0.5), 1)
                    s_mv2.wait(s, i + 1)
                    s.activation(lntmp4[:], mvx[1][:, 1:2], AF.Ln, bias=epsc[:, 0:1])
                    s.activation(rstdx[1][:], lntmp4[:], AF.Exp, scale=-0.5)
                    s_rs2.inc(s.activation(rstdx[1][:], lntmp4[:], AF.Exp, scale=-0.5), 1)
                    if i == 1 and debug_taps:
                        s_dbgf.wait(s, 176)
                    s_lgm.wait(s, i)
                    for grp in range(4):
                        gi = 4 * i + grp
                        s_htr.wait(s, gi + 1)
                        s_hev.inc(s.activation(hT[0][:, 1024 * grp:1024 * (grp + 1)],
                                               ps_htr[grp % 2][:, 0:1024], AF.Copy), 1)
                    s_lgr.wait(s, i + 1)
                    s.activation(ex[:], lg[:], AF.Exp, bias=negmx[:, 0:1],
                                 accum_out=sacc[:, 0:1])
                    s_ex.inc(s.activation(ex[:, 48:64], lg[:, 48:64], AF.Exp,
                                          bias=negmx[:, 0:1]), 1)
                    s_wtr.wait(s, i + 1)
                    s.activation(wnT[0:64, :], ps_wtr[0:64, :], AF.Copy)
                    s_wte.inc(s.activation(wnT[0:64, :], ps_wtr[0:64, :], AF.Copy), 1)
                    s_mv3.wait(s, i + 1)
                    s.activation(lntmp4[:], mvx[0][:, 1:2], AF.Ln, bias=epsc[:, 0:1])
                    s.activation(rstdx[0][:], lntmp4[:], AF.Exp, scale=-0.5)
                    s_rs3.inc(s.activation(rstdx[0][:], lntmp4[:], AF.Exp, scale=-0.5), 1)

            @blk.tensor
            def _(t):
                s_icast.wait(t, 32)
                for i in range(8):
                    s_hbf.wait(t, i + 1)
                    for grp in range(4):
                        gi = 4 * i + grp
                        if gi >= 2:
                            s_hev.wait(t, gi - 1)
                        for j in list(range(8)) + [7]:
                            cc = 8 * grp + j
                            last = t.transpose(ps_htr[grp % 2][:, 128 * j:128 * (j + 1)],
                                               h_bf[:, 128 * cc:128 * (cc + 1)], idn[:])
                        s_htr.inc(last, 1)
                    s_hev.wait(t, 4 * (i + 1))
                    if i >= 1:
                        s_lgr.wait(t, i)
                    for cc in range(NCC):
                        last = t.matmul(ps_log[:], hT[0][:, 128 * cc:128 * (cc + 1)],
                                        wr_sb[:, 64 * cc:64 * (cc + 1)],
                                        start=(cc == 0), stop=(cc == NCC - 1))
                    s_lgm.inc(last, 1)
                    s_wn.wait(t, i + 1)
                    if i >= 1:
                        s_wte.wait(t, i)
                    t.transpose(ps_wtr[0:64, 0:128], wn_bf[:, 0:64], idn[:])
                    s_wtr.inc(t.transpose(ps_wtr[0:64, 0:128], wn_bf[:, 0:64], idn[:]), 1)
                    s_wte.wait(t, i + 1)
                    for fb in range(8):
                        bmi = 8 * i + fb
                        if bmi >= 2:
                            s_fev.wait(t, bmi - 1)
                        s_bm.inc(t.matmul(ps_bm[fb % 2][:], wnT[0:64, :],
                                          eb_sb[0:64, 512 * fb:512 * (fb + 1)],
                                          start=True, stop=True), 1)

        assert s_out.cnt == (128 + 5 * 16 if debug_taps else 128) and s_fin.cnt == 8
        assert s_bm.cnt == 64 and s_fev.cnt == 64

    return nc


def _perm():
    p = np.empty(CD, dtype=np.int64)
    for j in range(32):
        R, hl = j // 8, j % 8
        kvl, g = hl // 2, hl % 2
        gh = 16 * g + 4 * R + kvl
        p[128 * j:128 * (j + 1)] = np.arange(128 * gh, 128 * (gh + 1))
    return p


_NC_CACHE = {}


def prepare_in_maps(L0_attn_x, L0_attn_Wq, L0_attn_bq, L0_attn_Wk, L0_attn_bk,
                    L0_attn_Wv, L0_attn_bv, L0_ffn_router_weight,
                    L0_ffn_router_bias, L0_ffn_expert_bias):
    x = np.asarray(L0_attn_x, dtype=np.float32)
    Wq = np.asarray(L0_attn_Wq, dtype=np.float32)
    bq = np.asarray(L0_attn_bq, dtype=np.float32)
    Wk = np.asarray(L0_attn_Wk, dtype=np.float32)
    bk = np.asarray(L0_attn_bk, dtype=np.float32)
    Wv = np.asarray(L0_attn_Wv, dtype=np.float32)
    bv = np.asarray(L0_attn_bv, dtype=np.float32)
    Wr = np.asarray(L0_ffn_router_weight, dtype=np.float32)
    br = np.asarray(L0_ffn_router_bias, dtype=np.float32)
    Eb = np.asarray(L0_ffn_expert_bias, dtype=np.float32)

    perm = _perm()

    Wq_p = Wq[:, perm]
    Wk_p = Wk[:, perm]
    Wv_p = Wv[:, perm]
    Wr_p = Wr[:, perm]
    Eb_p = np.ascontiguousarray(Eb[:, perm])

    wrt = np.ascontiguousarray(
        Wr_p.T.reshape(NCC, 128, E).transpose(1, 0, 2).reshape(128, NCC * E))
    br_bc = np.ascontiguousarray(np.broadcast_to(br[None, :], (128, E)))

    in_maps = []
    for c in range(8):
        b, r = c // 4, c % 4
        x_sl = np.ascontiguousarray(x[b, NTOK * r:NTOK * (r + 1), :][:, perm])
        wt = np.empty((NDS, 128, NCC * 128), dtype=np.float32)
        bias_t = np.empty((128, NDS), dtype=np.float32)
        for ds in range(NDS):
            if ds < 8:
                g, kvl = ds // 4, ds % 4
                h = 16 * g + 4 * r + kvl
                Wsub = Wq_p[128 * h:128 * (h + 1), :]
                bias_t[:, ds] = bq[128 * h:128 * (h + 1)]
            elif ds < 12:
                kvh = 4 * r + (ds - 8)
                Wsub = Wk_p[128 * kvh:128 * (kvh + 1), :]
                bias_t[:, ds] = bk[128 * kvh:128 * (kvh + 1)]
            else:
                kvh = 4 * r + (ds - 12)
                Wsub = Wv_p[128 * kvh:128 * (kvh + 1), :]
                bias_t[:, ds] = bv[128 * kvh:128 * (kvh + 1)]
            wt[ds] = Wsub.T.reshape(NCC, 128, 128).transpose(1, 0, 2).reshape(128, NCC * 128)
        sel = np.zeros((128, 4), dtype=np.float32)
        sel[:, r] = 1.0
        in_maps.append({
            "x_sh": x_sl,
            "wt_in": np.ascontiguousarray(wt),
            "bias_in": np.ascontiguousarray(bias_t),
            "wrt_in": wrt,
            "br_in": br_bc,
            "eb_in": Eb_p,
            "sel_in": sel,
        })

    return in_maps


def kernel(**inputs):
    perm = _perm()
    in_maps = prepare_in_maps(**inputs)
    if "nc" not in _NC_CACHE:
        _NC_CACHE["nc"] = build_program()
    nc = _NC_CACHE["nc"]

    res = run_bass_kernel_spmd(nc, in_maps, core_ids=list(range(8)))

    out = np.empty((B, T, CD), dtype=np.float32)
    for c in range(8):
        b, r = c // 4, c % 4
        tmp = np.empty((NTOK, CD), dtype=np.float32)
        tmp[:, perm] = np.asarray(res.results[c]["y_sh"])
        out[b, NTOK * r:NTOK * (r + 1), :] = tmp
    return out
